# version 20
# baseline (speedup 1.0000x reference)
import numpy as np
import concourse.bass as bass
import concourse.mybir as mybir
from concourse.bass_utils import run_bass_kernel_spmd

F32 = mybir.dt.float32
BF16 = mybir.dt.bfloat16
AF = mybir.ActivationFunctionType
OP = mybir.AluOpType

B, S, D = 4, 8192, 1024
H, HD = 16, 64
QH = S // 2
KC = D // 128
CH = 512
NCH = QH // CH
SCALE = HD ** -0.5
EPS_LN = 1e-5
EPS_W = 1e-6
QSPAN = 1024
NSPAN = QH // QSPAN
GKT = 4
GK = GKT * 128
NGRP = S // GK
NPAIR = KC
INV_D = 1.0 / D


class Sem:
    def __init__(self, handle):
        self.h = handle
        self.n = 0


class Prog:
    ENGS = ("tensor", "vector", "scalar", "sync", "gpsimd")

    def __init__(self, nc, pe, dve, act, W, G):
        self.nc = nc
        self.pe, self.dve, self.act, self.W, self.G = pe, dve, act, W, G
        self.ops = {e: [] for e in self.ENGS}

    def emit(self, eng, thunk, inc=None):
        self.ops[eng].append((thunk, inc))

    def inc(self, eng, thunk, sem, k=1):
        sem.n += k
        self.emit(eng, thunk, inc=(sem, k))
        return sem.n

    def wait(self, eng, sem, cnt):
        if cnt <= 0:
            return
        self.emit(eng, lambda e, s=sem.h, c=cnt: e.wait_ge(s, c))

    def load(self, dst, src):
        return self.inc("sync", lambda e, d=dst, s=src: e.dma_start(d, s), self.W, 16)

    def store(self, dst, src):
        return self.inc("gpsimd", lambda e, d=dst, s=src: e.dma_start(d, s), self.G, 16)

    def barrier(self):
        marks = [(self.pe, self.pe.n), (self.dve, self.dve.n), (self.act, self.act.n),
                 (self.W, self.W.n), (self.G, self.G.n)]
        for eng in self.ENGS:
            for sem, c in marks:
                self.wait(eng, sem, c)

    def run_block(self):
        with self.nc.Block() as block:
            for name in self.ENGS:
                opl = self.ops[name]

                def body(eng, opl=opl):
                    for thunk, inc in opl:
                        ins = thunk(eng)
                        if inc is not None:
                            ins.then_inc(inc[0].h, inc[1])

                getattr(block, name)(body)
        self.ops = {e: [] for e in self.ENGS}


def _hc(buf, hc, n=CH):
    return buf[:, hc * n:(hc + 1) * n]


def emit_ln(P, xin, bufs, psums, c_xin_ready):
    sqb, lnb = bufs["sqb"], bufs["lnb"]
    tmp, mbc, var, lnv, rstd = bufs["tmp"], bufs["mbc"], bufs["var"], bufs["lnv"], bufs["rstd"]
    ones, onesf = bufs["ones128b"], bufs["ones128f"]
    S1, S2 = psums["S1"], psums["S2"]

    P.wait("scalar", P.dve, c_xin_ready.get("dve", 0))
    P.wait("scalar", P.pe, c_xin_ready.get("pe", 0))
    P.wait("scalar", P.W, c_xin_ready.get("W", 0))
    c_sq = P.inc("scalar", lambda e, o=sqb[:, :], i=xin[:, :]: e.activation(o, i, AF.Square), P.act)
    P.wait("tensor", P.dve, c_xin_ready.get("dve", 0))
    P.wait("tensor", P.W, c_xin_ready.get("W", 0))
    P.wait("tensor", P.act, c_xin_ready.get("act", 0))
    for hc in range(KC):
        P.emit("tensor", lambda e, o=S1[:, :], l=onesf[:, :], r=_hc(xin, hc),
               st=(hc == 0), sp=(hc == KC - 1): e.matmul(o, l, r, start=st, stop=sp))
    P.wait("tensor", P.act, c_sq)
    c_stats = 0
    for hc in range(KC):
        c_stats = P.inc("tensor", lambda e, o=S2[:, :], l=ones[:, :], r=_hc(sqb, hc),
                        st=(hc == 0), sp=(hc == KC - 1): e.matmul(o, l, r, start=st, stop=sp), P.pe)
    P.wait("vector", P.pe, c_stats)
    P.emit("vector", lambda e, o=mbc[:, :], i=S1[:, :]: e.tensor_scalar_mul(o, i, INV_D))
    P.emit("vector", lambda e, o=tmp[:, :], a=mbc[:, :], b=mbc[:, :]: e.tensor_mul(o, a, b))
    P.emit("vector", lambda e, o=var[:, :], i0=S2[:, :], t=tmp[:, :]: e.scalar_tensor_tensor(
        o, i0, INV_D, t, OP.mult, OP.subtract))
    c_var = P.inc("vector", lambda e, o=var[:, :], i=var[:, :]: e.tensor_scalar_add(o, i, EPS_LN), P.dve)
    P.wait("scalar", P.dve, c_var)
    P.emit("scalar", lambda e, o=lnv[:, :], i=var[:, :]: e.activation(o, i, AF.Ln))
    c_rstd = P.inc("scalar", lambda e, o=rstd[:, :], i=lnv[:, :]: e.activation(o, i, AF.Exp, scale=-0.5), P.act)
    P.wait("vector", P.act, c_rstd)
    P.wait("vector", P.pe, c_xin_ready.get("pe_lnb", 0))
    c_lnb = 0
    for hc in range(KC):
        P.emit("vector", lambda e, o=tmp[:, :], a=_hc(xin, hc), b=mbc[:, :]: e.tensor_sub(o, a, b))
        c_lnb = P.inc("vector", lambda e, o=_hc(lnb, hc), a=tmp[:, :], b=rstd[:, :]: e.tensor_mul(o, a, b), P.dve)
    return {"c_lnb": c_lnb, "c_stats": c_stats, "c_var": c_var}


def emit_ffn(P, lnb, xin, xout, w_sb, bufs, psums, c_lnb, c_gfree=0):
    plb, dishab, p0b = bufs["plb"], bufs["dishab"], bufs["p0b"]
    PL, Fp = psums["PL"], psums["Fp"]
    P.wait("vector", P.G, c_gfree)
    P.wait("tensor", P.dve, c_lnb)
    c_pl = 0
    for hc in range(KC):
        c_pl = P.inc("tensor", lambda e, o=PL[:, :], l=p0b[:, :], r=_hc(lnb, hc),
                     st=(hc == 0), sp=(hc == KC - 1): e.matmul(o, l, r, start=st, stop=sp), P.pe)
    P.wait("vector", P.pe, c_pl)
    c_plb = P.inc("vector", lambda e, o=plb[:, :], i=PL[:, :]: e.tensor_copy(o, i), P.dve)
    P.wait("tensor", P.dve, c_plb)
    c_x1 = {}
    for oc in range(KC):
        Fv = Fp[:, (oc % 2) * 512:(oc % 2) * 512 + 512]
        if oc >= 2:
            P.wait("tensor", P.dve, c_x1[oc - 2])
        for hc in range(KC):
            P.emit("tensor", lambda e, o=Fv, l=w_sb[:, hc * D + oc * 128: hc * D + oc * 128 + 128],
                   r=_hc(lnb, hc), st=(hc == 0): e.matmul(o, l, r, start=st, stop=False))
        c_f = P.inc("tensor", lambda e, o=Fv, l=dishab[0:64, oc * 128:oc * 128 + 128],
                    r=plb[:, :]: e.matmul(o, l, r, start=False, stop=True), P.pe)
        P.wait("vector", P.pe, c_f)
        c_x1[oc] = P.inc("vector", lambda e, o=_hc(xout, oc), i0=Fv, i1=_hc(xin, oc): e.scalar_tensor_tensor(
            o, i0, 0.5, i1, OP.mult, OP.add), P.dve)
    return {"c_x1": c_x1[KC - 1], "c_pe_end": P.pe.n}


def emit_proj_fm(P, lnb, w_sb, stage, dst_dram_cols, P_, c_lnb, c_fp_free=0, c_gfree=0):
    Fp = P_["Fp"]
    P.wait("tensor", P.dve, c_lnb)
    P.wait("tensor", P.dve, c_fp_free)
    P.wait("vector", P.G, c_gfree)
    c_store = {}
    for oc in range(KC):
        Fv = Fp[:, (oc % 2) * 512:(oc % 2) * 512 + 512]
        if oc >= 2:
            P.wait("tensor", P.dve, c_store[oc - 2][0])
        c_f = 0
        for hc in range(KC):
            c_f = P.inc("tensor", lambda e, o=Fv, l=w_sb[:, hc * D + oc * 128: hc * D + oc * 128 + 128],
                        r=_hc(lnb, hc), st=(hc == 0), sp=(hc == KC - 1): e.matmul(o, l, r, start=st, stop=sp), P.pe)
        sv = stage[:, (oc % 2) * 512:(oc % 2) * 512 + 512]
        P.wait("vector", P.pe, c_f)
        if oc >= 2:
            P.wait("vector", P.G, c_store[oc - 2][1])
        c_ev = P.inc("vector", lambda e, o=sv, i=Fv: e.tensor_copy(o, i), P.dve)
        P.wait("gpsimd", P.dve, c_ev)
        c_st = P.store(dst_dram_cols(oc), sv)
        c_store[oc] = (c_ev, c_st)
    return c_store[KC - 1]


def build_nc(debug=False, stop_after=None):
    nc = bass.Bass("TRN2", target_bir_lowering=False)
    scr_kind = "ExternalOutput" if debug else "Internal"

    xfm_own = nc.dram_tensor("xfm_own", [D, QH], F32, kind="ExternalInput")
    xfm_oth = nc.dram_tensor("xfm_oth", [D, QH], F32, kind="ExternalInput")
    wts = {}
    for nm in ("wtq", "wtk", "wtv", "wto", "wtf"):
        wts[nm] = nc.dram_tensor(nm, [D, D], F32, kind="ExternalInput")
    disha_d = nc.dram_tensor("disha", [64, D], F32, kind="ExternalInput")
    p0_d = nc.dram_tensor("p0", [128, 64], F32, kind="ExternalInput")
    out_fm = nc.dram_tensor("out_fm", [D, QH], F32, kind="ExternalOutput")
    wbo_scr = nc.dram_tensor("wbo_scr", [D, D], BF16, kind=scr_kind)
    wbf_scr = nc.dram_tensor("wbf_scr", [D, D], BF16, kind=scr_kind)
    k_scr = nc.dram_tensor("k_scr", [D, S], BF16, kind=scr_kind)
    v_scr = nc.dram_tensor("v_scr", [S, D], BF16, kind=scr_kind)
    q_scr = nc.dram_tensor("q_scr", [D, QH], BF16, kind=scr_kind)
    x1_scr = nc.dram_tensor("x1_scr", [D, QH], F32, kind=scr_kind)
    if debug:
        x2_dbg = nc.dram_tensor("x2_dbg", [D, QH], F32, kind="ExternalOutput")
        z_dbg = nc.dram_tensor("z_dbg", [128, QH], F32, kind="ExternalOutput")
        aacc_dbg = nc.dram_tensor("aacc_dbg", [128, NPAIR * QSPAN], F32, kind="ExternalOutput")
        zacc_dbg = nc.dram_tensor("zacc_dbg", [128, NPAIR * QSPAN], F32, kind="ExternalOutput")
        abf_dbg = nc.dram_tensor("abf_dbg", [128, KC * CH], BF16, kind="ExternalOutput")

    from contextlib import ExitStack
    with ExitStack() as _st:
        ec = _st.enter_context
        h_pe = ec(nc.semaphore("s_pe"))
        h_dve = ec(nc.semaphore("s_dve"))
        h_act = ec(nc.semaphore("s_act"))
        h_w = ec(nc.semaphore("s_w"))
        h_g = ec(nc.semaphore("s_g"))
        xc = ec(nc.sbuf_tensor("xc", [128, KC * CH], F32))
        x1c = ec(nc.sbuf_tensor("x1c", [128, KC * CH], F32))
        sqb = ec(nc.sbuf_tensor("sqb", [128, KC * CH], BF16))
        lnb = ec(nc.sbuf_tensor("lnb", [128, KC * CH], BF16))
        tmp = ec(nc.sbuf_tensor("tmp", [128, CH], F32))
        mbc = ec(nc.sbuf_tensor("mbc", [128, CH], F32))
        var = ec(nc.sbuf_tensor("var", [128, CH], F32))
        lnv = ec(nc.sbuf_tensor("lnv", [128, CH], F32))
        rstd = ec(nc.sbuf_tensor("rstd", [128, CH], F32))
        plb = ec(nc.sbuf_tensor("plb", [64, CH], BF16))
        dishab = ec(nc.sbuf_tensor("dishab", [64, D], BF16))
        p0b = ec(nc.sbuf_tensor("p0b", [128, 64], BF16))
        ones128b = ec(nc.sbuf_tensor("ones128b", [128, 128], BF16))
        ones128f = ec(nc.sbuf_tensor("ones128f", [128, 128], F32))
        ones64b = ec(nc.sbuf_tensor("ones64b", [128, 64], BF16))
        pe, dve, act = Sem(h_pe), Sem(h_dve), Sem(h_act),
        W, G = Sem(h_w), Sem(h_g)
        P = Prog(nc, pe, dve, act, W, G)
        bufs = dict(sqb=sqb, lnb=lnb, ones128f=ones128f, tmp=tmp, mbc=mbc, var=var, lnv=lnv,
                    rstd=rstd, plb=plb, dishab=dishab, p0b=p0b, ones128b=ones128b)

        with ExitStack() as _stp:
            ecp = _stp.enter_context
            w_f = ecp(nc.sbuf_tensor("w_f", [128, KC * D], BF16))
            xc2 = ecp(nc.sbuf_tensor("xc2", [128, KC * CH], F32))
            x1c2 = ecp(nc.sbuf_tensor("x1c2", [128, KC * CH], F32))
            w_q = ecp(nc.sbuf_tensor("w_q", [128, KC * D], BF16))
            w_k = ecp(nc.sbuf_tensor("w_k", [128, KC * D], BF16))
            w_v = ecp(nc.sbuf_tensor("w_v", [128, KC * D], BF16))
            wstage = ecp(nc.sbuf_tensor("wstage", [128, 2048], F32))
            qstage = ecp(nc.sbuf_tensor("qstage", [128, 1024], BF16))
            kstage = ecp(nc.sbuf_tensor("kstage", [128, 1024], BF16))
            vstage = ecp(nc.sbuf_tensor("vstage", [128, 2048], BF16))
            Fp = ecp(nc.psum_tensor("Fp", [128, 1024], F32))
            Vp = ecp(nc.psum_tensor("Vp", [128, 1024], F32))
            psums = dict(S1=Fp[:, 0:512], S2=Fp[:, 512:1024], PL=Vp[0:64, 0:512], Fp=Fp)

            P.emit("vector", lambda e, o=ones128b[:, :]: e.memset(o, 1.0))
            P.emit("vector", lambda e, o=ones128f[:, :]: e.memset(o, 1.0))
            c_ones = P.inc("vector", lambda e, o=ones64b[:, :]: e.memset(o, 1.0), P.dve)

            w_map = [(wts["wtq"], w_q, None), (wts["wtk"], w_k, None),
                     (wts["wtv"], w_v, None), (wts["wtf"], w_f, wbf_scr),
                     (wts["wto"], None, wbo_scr)]
            slot = 0
            c_cast_of_slot = {}
            for wd, wsb, scr in w_map:
                wdr = wd.rearrange("(hc p) o -> hc p o", p=128)
                for hc in range(KC):
                    half = slot % 2
                    if slot >= 2:
                        P.wait("gpsimd", P.dve, c_cast_of_slot[slot - 2])
                    c_l = P.store(wstage[:, half * 1024:half * 1024 + 1024], wdr[hc])
                    P.wait("vector", P.G, c_l)
                    if wsb is not None:
                        dst = wsb[:, hc * D:(hc + 1) * D]
                    else:
                        P.wait("vector", P.G, G.n)
                        dst = qstage[:, 0:1024]
                    c_cast_of_slot[slot] = P.inc(
                        "vector", lambda e, o=dst,
                        i=wstage[:, half * 1024:half * 1024 + 1024]: e.tensor_copy(o, i), P.dve)
                    if scr is not None:
                        P.wait("gpsimd", P.dve, c_cast_of_slot[slot])
                        P.store(scr[hc * 128:(hc + 1) * 128, :], dst)
                    slot += 1
            P.wait("gpsimd", P.dve, c_cast_of_slot[slot - 1])
            c_l = P.store(wstage[0:64, 0:1024], disha_d[:, :])
            P.wait("vector", P.G, c_l)
            P.inc("vector", lambda e, o=dishab[:, :], i=wstage[0:64, 0:1024]: e.tensor_copy(o, i), P.dve)
            c_l = P.store(wstage[:, 1024:1088], p0_d[:, :])
            P.wait("vector", P.G, c_l)
            c_wts = P.inc("vector", lambda e, o=p0b[:, :], i=wstage[:, 1024:1088]: e.tensor_copy(o, i), P.dve)

            pv = dict(c_x1=0, c_sq1=0, c_stats2=0, c_var2=c_wts, c_vpe=0, c_vst3=0,
                      c_qst=0, c_kst=0, c_vst=0, c_x1st=0, c_x1_2=0, c_sq1_2=0)
            ci = 0
            for half, src_ in (("own", xfm_own), ("oth", xfm_oth)):
                srcr = src_.rearrange("(hc p) t -> hc p t", p=128)
                for c in range(NCH):
                    xc_c = xc if ci % 2 == 0 else xc2
                    x1c_c = x1c if ci % 2 == 0 else x1c2
                    ci += 1
                    tok_l = c * CH
                    key0 = (0 if half == "own" else QH) + tok_l
                    P.wait("sync", P.dve, pv["c_x1_2"])
                    P.wait("sync", P.act, pv["c_sq1_2"])
                    c_x = 0
                    for hc in range(KC):
                        c_x = P.load(_hc(xc_c, hc), srcr[hc][:, tok_l:tok_l + CH])
                    r1 = emit_ln(P, xc_c, bufs, psums,
                                 {"W": c_x, "dve": pv["c_var2"], "pe": pv["c_stats2"],
                                  "pe_lnb": pv["c_vpe"]})
                    c_sq1 = P.act.n - 1
                    r2 = emit_ffn(P, lnb, xc_c, x1c_c, w_f, bufs, psums, r1["c_lnb"],
                                  c_gfree=pv["c_x1st"])
                    c_x1st = pv["c_x1st"]
                    if half == "own":
                        x1r = x1_scr.rearrange("(hc p) t -> hc p t", p=128)
                        P.wait("gpsimd", P.dve, r2["c_x1"])
                        for hc in range(KC):
                            c_x1st = P.store(x1r[hc][:, tok_l:tok_l + CH], _hc(x1c_c, hc))
                    r3 = emit_ln(P, x1c_c, bufs, psums,
                                 {"dve": r2["c_x1"], "pe": r1["c_stats"], "pe_lnb": 0})
                    c_lnb = r3["c_lnb"]
                    c_prev_evac = r2["c_x1"]
                    c_qst = pv["c_qst"]
                    if half == "own":
                        qr = q_scr.rearrange("(oc p) t -> oc p t", p=128)
                        c_qev, c_qst = emit_proj_fm(P, lnb, w_q, qstage,
                                                    lambda oc, qr=qr, t=tok_l: qr[oc][:, t:t + CH],
                                                    psums, c_lnb, r2["c_x1"], c_gfree=pv["c_qst"])
                    kr = k_scr.rearrange("(oc p) t -> oc p t", p=128)
                    c_kev, c_kst = emit_proj_fm(P, lnb, w_k, kstage,
                                                lambda oc, kr=kr, t=key0: kr[oc][:, t:t + CH],
                                                psums, c_lnb, P.dve.n, c_gfree=pv["c_kst"])
                    P.wait("tensor", P.dve, c_lnb)
                    P.wait("tensor", P.dve, pv["c_vst3"])
                    P.wait("vector", P.G, pv["c_vst"])
                    c_vst = {}
                    c_vpe = 0
                    for tt in range(4):
                        if tt >= 1:
                            P.wait("tensor", P.dve, c_vst[tt - 1][0])
                        c_v = 0
                        for hc in range(KC):
                            for on in range(2):
                                c_v = P.inc("tensor", lambda e, o=Vp[:, on * 512:on * 512 + 512],
                                            l=lnb[:, hc * CH + tt * 128: hc * CH + tt * 128 + 128],
                                            r=w_v[:, hc * D + on * 512: hc * D + on * 512 + 512],
                                            st=(hc == 0), sp=(hc == KC - 1):
                                            e.matmul(o, l, r, start=st, stop=sp), P.pe)
                        c_vpe = c_v
                        sv = vstage[:, (tt % 2) * 1024:(tt % 2) * 1024 + 1024]
                        P.wait("vector", P.pe, c_v)
                        if tt >= 2:
                            P.wait("vector", P.G, c_vst[tt - 2][1])
                        c_ev = P.inc("vector", lambda e, o=sv, i=Vp[:, :]: e.tensor_copy(o, i), P.dve)
                        P.wait("gpsimd", P.dve, c_ev)
                        c_st = P.store(v_scr[key0 + tt * 128: key0 + tt * 128 + 128, :], sv)
                        c_vst[tt] = (c_ev, c_st)
                    pv = dict(c_x1=r2["c_x1"], c_sq1=c_sq1, c_stats2=r3["c_stats"],
                              c_var2=max(r3["c_var"], c_kev), c_vpe=c_vpe, c_vst3=c_vst[3][0],
                              c_qst=c_qst, c_kst=c_kst, c_vst=c_vst[3][1], c_x1st=c_x1st,
                              c_x1_2=pv["c_x1"], c_sq1_2=pv["c_sq1"])
            P.barrier()
            P.run_block()

        if stop_after == "pre":
            P.inc("gpsimd", lambda e: e.dma_start(out_fm[0:128, 0:CH], xc[:, 0:CH]), G, 16)
            P.wait("gpsimd", G, G.n)
            P.run_block()
            return nc

        with ExitStack() as _sta:
            eca = _sta.enter_context
            qspan = eca(nc.sbuf_tensor("qspan", [128, NPAIR * QSPAN], BF16))
            kg = eca(nc.sbuf_tensor("kg", [128, 2 * KC * GK], BF16))
            vg = eca(nc.sbuf_tensor("vg", [128, 2 * GKT * D], BF16))
            wbuf = eca(nc.sbuf_tensor("wbuf", [128, 4 * 2 * QSPAN], BF16))
            attn_sb = eca(nc.sbuf_tensor("attn_sb", [128, NPAIR * QSPAN], F32))
            abf0 = eca(nc.sbuf_tensor("abf0", [128, KC * CH], BF16))
            abf1 = eca(nc.sbuf_tensor("abf1", [128, KC * CH], BF16))
            z_sb = eca(nc.sbuf_tensor("z_sb", [128, NPAIR * QSPAN], F32))
            scores = eca(nc.psum_tensor("scores", [128, 2 * QSPAN], F32))
            av = eca(nc.psum_tensor("av", [128, QSPAN], F32))
            zps = eca(nc.psum_tensor("zps", [128, QSPAN], F32))
            Wo0 = av[:, 0:512]
            Wo1 = av[:, 512:1024]
            tail_psums = dict(S1=scores[:, 1024:1536], S2=scores[:, 1536:2048],
                              PL=zps[0:64, 0:512], Fp=scores[:, 0:1024])
            kgr = k_scr.rearrange("(hc p) t -> hc p t", p=128)
            qr = q_scr.rearrange("(oc p) t -> oc p t", p=128)
            x1r = x1_scr.rearrange("(hc p) t -> hc p t", p=128)
            outr = out_fm.rearrange("(oc p) t -> oc p t", p=128)

            sigA_cnt, sigB_cnt = {}, {}
            scr_cnt = {}
            avz_cnt = {}
            kv_cnt = {}
            kti = 0
            c_tail_dve = 0
            c_tail_pe = 0
            tail_gates = {"c_stats3": 0, "c_nrm0": 0, "c_wo_last": 0}
            c_q_w = 0
            pend = []
            spill_cnt = {"av": 0, "z": 0}

            def emit_avz(p):
                ps_, pg, ppr, pk4, pkti, pgb = p
                P.wait("tensor", P.act, sigB_cnt[pkti])
                if pk4 == 0:
                    P.wait("tensor", P.dve, spill_cnt["av"])
                wb = wbuf[:, (pkti % 4) * 2 * QSPAN:(pkti % 4 + 1) * 2 * QSPAN]
                vA = vg[:, pgb * GKT * D + pk4 * D + ppr * 128: pgb * GKT * D + pk4 * D + ppr * 128 + 64]
                vB = vg[:, pgb * GKT * D + pk4 * D + ppr * 128 + 64: pgb * GKT * D + pk4 * D + ppr * 128 + 128]
                st0, sp0 = (pk4 == 0), (pk4 == GKT - 1)
                for qs in range(2):
                    wA = wb[:, qs * 512:qs * 512 + 512]
                    wB = wb[:, 1024 + qs * 512:1024 + qs * 512 + 512]
                    P.emit("tensor", lambda e, o=av[0:64, qs * 512:qs * 512 + 512], l=vA, r=wA,
                           st=st0, sp=sp0: e.matmul(o, l, r, start=st, stop=sp, tile_position=(0, 0)))
                    P.emit("tensor", lambda e, o=av[64:128, qs * 512:qs * 512 + 512], l=vB, r=wB,
                           st=st0, sp=sp0: e.matmul(o, l, r, start=st, stop=sp, tile_position=(0, 64)))
                if pk4 == 0:
                    P.wait("tensor", P.dve, spill_cnt["z"])
                c_avz = 0
                for qs in range(2):
                    wA = wb[:, qs * 512:qs * 512 + 512]
                    wB = wb[:, 1024 + qs * 512:1024 + qs * 512 + 512]
                    P.emit("tensor", lambda e, o=zps[0:64, qs * 512:qs * 512 + 512], l=ones64b[:, :], r=wA,
                           st=st0, sp=sp0: e.matmul(o, l, r, start=st, stop=sp, tile_position=(0, 0)))
                    c_avz = P.inc("tensor", lambda e, o=zps[64:128, qs * 512:qs * 512 + 512],
                                  l=ones64b[:, :], r=wB,
                                  st=st0, sp=sp0: e.matmul(o, l, r, start=st, stop=sp, tile_position=(0, 64)), P.pe)
                avz_cnt[pkti] = c_avz
                if sp0:
                    P.wait("vector", P.pe, c_avz)
                    asl = attn_sb[:, ppr * QSPAN:(ppr + 1) * QSPAN]
                    zsl = z_sb[:, ppr * QSPAN:(ppr + 1) * QSPAN]
                    if pg == 0:
                        spill_cnt["av"] = P.inc("vector", lambda e, o=asl, i=av[:, :]: e.tensor_copy(o, i), P.dve)
                        spill_cnt["z"] = P.inc("vector", lambda e, o=zsl, i=zps[:, :]: e.tensor_copy(o, i), P.dve)
                    else:
                        spill_cnt["av"] = P.inc("vector", lambda e, o=asl, a=asl, b=av[:, :]: e.tensor_add(o, a, b), P.dve)
                        spill_cnt["z"] = P.inc("vector", lambda e, o=zsl, a=zsl, b=zps[:, :]: e.tensor_add(o, a, b), P.dve)
                    if pg == NGRP - 1:
                        P.wait("vector", P.pe, tail_gates["c_wo_last"])
                        P.emit("vector", lambda e, o=zsl, i=zsl: e.tensor_scalar_add(o, i, EPS_W))
                        P.emit("vector", lambda e, o=zsl, i=zsl: e.reciprocal(o, i))
                        P.emit("vector", lambda e, o=_hc(abf0, ppr),
                               a=attn_sb[:, ppr * QSPAN: ppr * QSPAN + 512],
                               b=z_sb[:, ppr * QSPAN: ppr * QSPAN + 512]: e.tensor_mul(o, a, b))
                        tail_gates["c_nrm0"] = P.inc(
                            "vector", lambda e, o=_hc(abf1, ppr),
                            a=attn_sb[:, ppr * QSPAN + 512: ppr * QSPAN + 1024],
                            b=z_sb[:, ppr * QSPAN + 512: ppr * QSPAN + 1024]: e.tensor_mul(o, a, b), P.dve)

            for s in range(NSPAN):
                q0 = s * QSPAN
                if s > 0:
                    P.wait("sync", P.pe, scr_cnt[kti - 1])
                for pr in range(NPAIR):
                    c_q_w = P.load(qspan[:, pr * QSPAN:(pr + 1) * QSPAN], qr[pr][:, q0:q0 + QSPAN])
                for g_pre in (0, 1):
                    gb = g_pre % 2
                    if s > 0:
                        P.wait("sync", P.pe, c_tail_pe)
                    for hc in range(KC):
                        P.load(kg[:, gb * KC * GK + hc * GK: gb * KC * GK + (hc + 1) * GK],
                               kgr[hc][:, g_pre * GK:(g_pre + 1) * GK])
                    c_kv = 0
                    for tt in range(GKT):
                        c_kv = P.load(vg[:, gb * GKT * D + tt * D: gb * GKT * D + (tt + 1) * D],
                                      v_scr[g_pre * GK + tt * 128: g_pre * GK + (tt + 1) * 128, :])
                    kv_cnt[(s, g_pre)] = c_kv

                c_wt = 0
                for g in range(NGRP):
                    gb = g % 2
                    for pr in range(NPAIR):
                        for k4 in range(GKT):
                            kt = g * GKT + k4
                            P.wait("tensor", P.act, sigA_cnt.get(kti - 1, 0))
                            P.wait("tensor", P.W, kv_cnt[(s, g)])
                            P.wait("tensor", P.W, c_q_w)
                            if g == 0 and pr == 0 and k4 == 0:
                                P.wait("tensor", P.dve, c_tail_dve)
                            kA = kg[0:64, gb * KC * GK + pr * GK + k4 * 128: gb * KC * GK + pr * GK + k4 * 128 + 128]
                            kB = kg[64:128, gb * KC * GK + pr * GK + k4 * 128: gb * KC * GK + pr * GK + k4 * 128 + 128]
                            c_scA = c_scB = 0
                            for qs in range(2):
                                qA = qspan[0:64, pr * QSPAN + qs * 512: pr * QSPAN + qs * 512 + 512]
                                c_scA = P.inc("tensor", lambda e, o=scores[:, qs * 512:qs * 512 + 512],
                                              l=kA, r=qA: e.matmul(o, l, r, start=True, stop=True, tile_position=(0, 0)), P.pe)
                            P.wait("tensor", P.act, sigB_cnt.get(kti - 1, 0))
                            for qs in range(2):
                                qB = qspan[64:128, pr * QSPAN + qs * 512: pr * QSPAN + qs * 512 + 512]
                                c_scB = P.inc("tensor", lambda e, o=scores[:, 1024 + qs * 512:1024 + qs * 512 + 512],
                                              l=kB, r=qB, last=(qs == 1): e.matmul(o, l, r, start=True, stop=True, tile_position=(64, 0)), P.pe)
                            scr_cnt[kti] = c_scB
                            P.wait("scalar", P.pe, c_scA)
                            if kti >= 4:
                                P.wait("scalar", P.pe, avz_cnt[kti - 4])
                            wb = wbuf[:, (kti % 4) * 2 * QSPAN:(kti % 4 + 1) * 2 * QSPAN]
                            sigA_cnt[kti] = P.inc("scalar", lambda e, o=wb[:, 0:1024], i=scores[:, 0:1024]: e.activation(
                                o, i, AF.Sigmoid, scale=SCALE), P.act)
                            P.wait("scalar", P.pe, c_scB)
                            sigB_cnt[kti] = P.inc("scalar", lambda e, o=wb[:, 1024:2048], i=scores[:, 1024:2048]: e.activation(
                                o, i, AF.Sigmoid, scale=SCALE), P.act)
                            if len(pend) >= 2:
                                emit_avz(pend.pop(0))
                            pend.append((s, g, pr, k4, kti, gb))
                            kti += 1
                    while pend:
                        emit_avz(pend.pop(0))
                    if g == NGRP - 2 or g == NGRP - 1:
                        P.wait("sync", P.pe, avz_cnt[kti - 1])
                        h0 = 0 if g == NGRP - 2 else 4
                        for hc in range(h0, h0 + 4):
                            P.load(kg[:, hc * D:(hc + 1) * D], wbo_scr[hc * 128:(hc + 1) * 128, :])
                            c_wt = P.load(vg[:, hc * D:(hc + 1) * D], wbf_scr[hc * 128:(hc + 1) * 128, :])
                    if g + 2 < NGRP:
                        P.wait("sync", P.pe, avz_cnt[kti - 1])
                        for hc in range(KC):
                            P.load(kg[:, gb * KC * GK + hc * GK: gb * KC * GK + (hc + 1) * GK],
                                   kgr[hc][:, (g + 2) * GK:(g + 3) * GK])
                        c_kv = 0
                        for tt in range(GKT):
                            c_kv = P.load(vg[:, gb * GKT * D + tt * D: gb * GKT * D + (tt + 1) * D],
                                          v_scr[(g + 2) * GK + tt * 128: (g + 2) * GK + (tt + 1) * 128, :])
                        kv_cnt[(s, g + 2)] = c_kv

                c_last_sig = sigB_cnt[kti - 1]
                c_spill = P.dve.n
                w_o_t, w_f_t = kg, vg
                if debug and s == 0:
                    P.wait("gpsimd", P.dve, c_spill)
                    P.store(aacc_dbg[:, :], attn_sb[:, :])
                    P.store(zacc_dbg[:, :], z_sb[:, :])
                c_ffn_dve_prev = 0
                c_store_prev = 0
                c_wo_pe = 0
                for sub in range(2):
                    t0 = q0 + sub * 512
                    abf = abf0 if sub == 0 else abf1
                    c_nrm = tail_gates["c_nrm0"]
                    if debug and s == 0 and sub == 0:
                        P.wait("gpsimd", P.dve, c_nrm)
                        P.store(abf_dbg[:, :], sqb[:, :])
                    P.wait("sync", P.dve, c_ffn_dve_prev)
                    c_x1l = 0
                    for hc in range(KC):
                        c_x1l = P.load(_hc(xc, hc), x1r[hc][:, t0:t0 + 512])
                    P.wait("tensor", P.dve, c_nrm)
                    P.wait("tensor", P.act, c_last_sig)
                    P.wait("tensor", P.W, c_wt)
                    P.wait("tensor", P.G, c_store_prev)
                    c_x2 = {}
                    P.wait("vector", P.G, c_store_prev)
                    for oc in range(KC):
                        Fv = Wo0 if oc % 2 == 0 else Wo1
                        if oc >= 2:
                            P.wait("tensor", P.dve, c_x2[oc - 2])
                        c_f = 0
                        for hc in range(KC):
                            c_f = P.inc("tensor", lambda e, o=w_o_t[:, hc * D + oc * 128: hc * D + oc * 128 + 128], r=_hc(abf, hc), oo=Fv,
                                        st=(hc == 0), sp=(hc == KC - 1): e.matmul(oo, o, r, start=st, stop=sp), P.pe)
                        P.wait("vector", P.pe, c_f)
                        P.wait("vector", P.W, c_x1l)
                        c_x2[oc] = P.inc("vector", lambda e, o=_hc(x1c, oc), a=_hc(xc, oc), b=Fv:
                                         e.tensor_add(o, a, b), P.dve)
                    c_wo_pe = P.pe.n
                    if sub == 1:
                        tail_gates["c_wo_last"] = c_wo_pe
                    if debug:
                        x2r = x2_dbg.rearrange("(hc p) t -> hc p t", p=128)
                        P.wait("gpsimd", P.dve, c_x2[KC - 1])
                        for oc in range(KC):
                            P.store(x2r[oc][:, t0:t0 + CH], _hc(x1c, oc))
                        P.store(z_dbg[:, t0:t0 + CH], z_sb[:, 0 * QSPAN + sub * 512: 0 * QSPAN + sub * 512 + 512])
                    P.wait("scalar", P.pe, c_wo_pe)
                    r1 = emit_ln(P, x1c, bufs, tail_psums, {"dve": c_x2[KC - 1]})
                    c_stats3 = r1["c_stats"]
                    r2 = emit_ffn(P, lnb, x1c, x1c, w_f_t, bufs, tail_psums, r1["c_lnb"],
                                  c_gfree=c_store_prev)
                    c_ffn_dve_prev = r2["c_x1"]
                    P.wait("gpsimd", P.dve, r2["c_x1"])
                    c_store_prev = 0
                    for oc in range(KC):
                        c_store_prev = P.store(outr[oc][:, t0:t0 + 512], _hc(x1c, oc))
                c_tail_dve = P.dve.n
                c_tail_pe = P.pe.n
                tail_gates["c_stats3"] = c_stats3

            P.wait("gpsimd", P.G, P.G.n)
            P.run_block()
    return nc


def kernel(L0_attn_x, L0_attn_Wq, L0_attn_Wk, L0_attn_Wv, L0_attn_Wo,
           L0_ffn_weight, L0_ffn_disha):
    x = np.asarray(L0_attn_x, dtype=np.float32)
    wq = np.ascontiguousarray(np.asarray(L0_attn_Wq, dtype=np.float32).T)
    wk = np.ascontiguousarray(np.asarray(L0_attn_Wk, dtype=np.float32).T)
    wv = np.ascontiguousarray(np.asarray(L0_attn_Wv, dtype=np.float32).T)
    wo = np.ascontiguousarray(np.asarray(L0_attn_Wo, dtype=np.float32).T)
    wf = np.ascontiguousarray(np.asarray(L0_ffn_weight, dtype=np.float32).T)
    disha = np.ascontiguousarray(np.asarray(L0_ffn_disha, dtype=np.float32))
    p0 = np.ascontiguousarray(np.tile(np.eye(64, dtype=np.float32), (2, 1)))

    nc = build_nc()
    in_maps = []
    for core in range(8):
        b, h = core // 2, core % 2
        xo = np.ascontiguousarray(x[b, h * QH:(h + 1) * QH, :].T)
        xt = np.ascontiguousarray(x[b, (1 - h) * QH:(2 - h) * QH, :].T)
        in_maps.append({"xfm_own": xo, "xfm_oth": xt, "wtq": wq, "wtk": wk,
                        "wtv": wv, "wto": wo, "wtf": wf, "disha": disha, "p0": p0})
    res = run_bass_kernel_spmd(nc, in_maps, core_ids=list(range(8)))
    out = np.empty((B, S, D), dtype=np.float32)
    for core in range(8):
        b, h = core // 2, core % 2
        out[b, h * QH:(h + 1) * QH, :] = res.results[core]["out_fm"].T
    return out


# version 22
# speedup vs baseline: 1.1940x; 1.1940x over previous
import numpy as np
import concourse.bass as bass
import concourse.mybir as mybir
from concourse.bass_utils import run_bass_kernel_spmd

F32 = mybir.dt.float32
BF16 = mybir.dt.bfloat16
AF = mybir.ActivationFunctionType
OP = mybir.AluOpType

B, S, D = 4, 8192, 1024
H, HD = 16, 64
QH = S // 2
KC = D // 128
CH = 512
NCH = QH // CH
SCALE = HD ** -0.5
EPS_LN = 1e-5
EPS_W = 1e-6
QSPAN = 1024
NSPAN = QH // QSPAN
GKT = 4
GK = GKT * 128
NGRP = S // GK
NPAIR = KC
INV_D = 1.0 / D


class Sem:
    def __init__(self, handle):
        self.h = handle
        self.n = 0


class Prog:
    ENGS = ("tensor", "vector", "scalar", "sync", "gpsimd")

    def __init__(self, nc, pe, dve, act, W, G):
        self.nc = nc
        self.pe, self.dve, self.act, self.W, self.G = pe, dve, act, W, G
        self.ops = {e: [] for e in self.ENGS}

    def emit(self, eng, thunk, inc=None):
        self.ops[eng].append((thunk, inc))

    def inc(self, eng, thunk, sem, k=1):
        sem.n += k
        self.emit(eng, thunk, inc=(sem, k))
        return sem.n

    def wait(self, eng, sem, cnt):
        if cnt <= 0:
            return
        self.emit(eng, lambda e, s=sem.h, c=cnt: e.wait_ge(s, c))

    def load(self, dst, src):
        return self.inc("sync", lambda e, d=dst, s=src: e.dma_start(d, s), self.W, 16)

    def store(self, dst, src):
        return self.inc("gpsimd", lambda e, d=dst, s=src: e.dma_start(d, s), self.G, 16)

    def barrier(self):
        marks = [(self.pe, self.pe.n), (self.dve, self.dve.n), (self.act, self.act.n),
                 (self.W, self.W.n), (self.G, self.G.n)]
        for eng in self.ENGS:
            for sem, c in marks:
                self.wait(eng, sem, c)

    def run_block(self):
        with self.nc.Block() as block:
            for name in self.ENGS:
                opl = self.ops[name]

                def body(eng, opl=opl):
                    for thunk, inc in opl:
                        ins = thunk(eng)
                        if inc is not None:
                            ins.then_inc(inc[0].h, inc[1])

                getattr(block, name)(body)
        self.ops = {e: [] for e in self.ENGS}


def _hc(buf, hc, n=CH):
    return buf[:, hc * n:(hc + 1) * n]


def emit_ln(P, xin, bufs, psums, c_xin_ready):
    sqb, lnb, xbb = bufs["sqb"], bufs["lnb"], bufs["xbb"]
    tmp, mbc, var, lnv, rstd = bufs["tmp"], bufs["mbc"], bufs["var"], bufs["lnv"], bufs["rstd"]
    ones = bufs["ones128b"]
    S1, S2 = psums["S1"], psums["S2"]

    P.wait("vector", P.dve, c_xin_ready.get("dve", 0))
    P.wait("vector", P.pe, c_xin_ready.get("pe", 0))
    P.wait("vector", P.W, c_xin_ready.get("W", 0))
    P.emit("vector", lambda e, o=xbb[:, :], i=xin[:, :]: e.tensor_copy(o, i))
    c_sq = P.inc("vector", lambda e, o=sqb[:, :], a=xbb[:, :], b=xbb[:, :]: e.tensor_mul(o, a, b), P.dve)
    P.wait("tensor", P.dve, c_sq)
    for hc in range(KC):
        P.emit("tensor", lambda e, o=S1[:, :], l=ones[:, :], r=_hc(xbb, hc),
               st=(hc == 0), sp=(hc == KC - 1): e.matmul(o, l, r, start=st, stop=sp))
    c_stats = 0
    for hc in range(KC):
        c_stats = P.inc("tensor", lambda e, o=S2[:, :], l=ones[:, :], r=_hc(sqb, hc),
                        st=(hc == 0), sp=(hc == KC - 1): e.matmul(o, l, r, start=st, stop=sp), P.pe)
    P.wait("vector", P.pe, c_stats)
    P.emit("vector", lambda e, o=mbc[:, :], i=S1[:, :]: e.tensor_scalar_mul(o, i, INV_D))
    P.emit("vector", lambda e, o=tmp[:, :], a=mbc[:, :], b=mbc[:, :]: e.tensor_mul(o, a, b))
    P.emit("vector", lambda e, o=var[:, :], i0=S2[:, :], t=tmp[:, :]: e.scalar_tensor_tensor(
        o, i0, INV_D, t, OP.mult, OP.subtract))
    c_var = P.inc("vector", lambda e, o=var[:, :], i=var[:, :]: e.tensor_scalar_add(o, i, EPS_LN), P.dve)
    P.wait("scalar", P.dve, c_var)
    P.emit("scalar", lambda e, o=lnv[:, :], i=var[:, :]: e.activation(o, i, AF.Ln))
    c_rstd = P.inc("scalar", lambda e, o=rstd[:, :], i=lnv[:, :]: e.activation(o, i, AF.Exp, scale=-0.5), P.act)
    P.wait("vector", P.act, c_rstd)
    P.wait("vector", P.pe, c_xin_ready.get("pe_lnb", 0))
    c_lnb = 0
    for hc in range(KC):
        P.emit("vector", lambda e, o=tmp[:, :], a=_hc(xin, hc), b=mbc[:, :]: e.tensor_sub(o, a, b))
        c_lnb = P.inc("vector", lambda e, o=_hc(lnb, hc), a=tmp[:, :], b=rstd[:, :]: e.tensor_mul(o, a, b), P.dve)
    return {"c_lnb": c_lnb, "c_stats": c_stats, "c_var": c_var}


def emit_ffn(P, lnb, xin, xout, w_sb, bufs, psums, c_lnb, c_gfree=0):
    plb, dishab, p0b = bufs["plb"], bufs["dishab"], bufs["p0b"]
    PL, Fp = psums["PL"], psums["Fp"]
    P.wait("vector", P.G, c_gfree)
    P.wait("tensor", P.dve, c_lnb)
    c_pl = 0
    for hc in range(KC):
        c_pl = P.inc("tensor", lambda e, o=PL[:, :], l=p0b[:, :], r=_hc(lnb, hc),
                     st=(hc == 0), sp=(hc == KC - 1): e.matmul(o, l, r, start=st, stop=sp), P.pe)
    P.wait("vector", P.pe, c_pl)
    c_plb = P.inc("vector", lambda e, o=plb[:, :], i=PL[:, :]: e.tensor_copy(o, i), P.dve)
    P.wait("tensor", P.dve, c_plb)
    c_x1 = {}
    for oc in range(KC):
        Fv = Fp[:, (oc % 2) * 512:(oc % 2) * 512 + 512]
        if oc >= 2:
            P.wait("tensor", P.dve, c_x1[oc - 2])
        for hc in range(KC):
            P.emit("tensor", lambda e, o=Fv, l=w_sb[:, hc * D + oc * 128: hc * D + oc * 128 + 128],
                   r=_hc(lnb, hc), st=(hc == 0): e.matmul(o, l, r, start=st, stop=False))
        c_f = P.inc("tensor", lambda e, o=Fv, l=dishab[0:64, oc * 128:oc * 128 + 128],
                    r=plb[:, :]: e.matmul(o, l, r, start=False, stop=True), P.pe)
        P.wait("vector", P.pe, c_f)
        c_x1[oc] = P.inc("vector", lambda e, o=_hc(xout, oc), i0=Fv, i1=_hc(xin, oc): e.scalar_tensor_tensor(
            o, i0, 0.5, i1, OP.mult, OP.add), P.dve)
    return {"c_x1": c_x1[KC - 1], "c_pe_end": P.pe.n}


def emit_proj_fm(P, lnb, w_sb, stage, dst_dram_cols, P_, c_lnb, c_fp_free=0, c_gfree=0):
    Fp = P_["Fp"]
    P.wait("tensor", P.dve, c_lnb)
    P.wait("tensor", P.dve, c_fp_free)
    P.wait("vector", P.G, c_gfree)
    c_store = {}
    for oc in range(KC):
        Fv = Fp[:, (oc % 2) * 512:(oc % 2) * 512 + 512]
        if oc >= 2:
            P.wait("tensor", P.dve, c_store[oc - 2][0])
        c_f = 0
        for hc in range(KC):
            c_f = P.inc("tensor", lambda e, o=Fv, l=w_sb[:, hc * D + oc * 128: hc * D + oc * 128 + 128],
                        r=_hc(lnb, hc), st=(hc == 0), sp=(hc == KC - 1): e.matmul(o, l, r, start=st, stop=sp), P.pe)
        sv = stage[:, (oc % 2) * 512:(oc % 2) * 512 + 512]
        P.wait("vector", P.pe, c_f)
        if oc >= 2:
            P.wait("vector", P.G, c_store[oc - 2][1])
        c_ev = P.inc("vector", lambda e, o=sv, i=Fv: e.tensor_copy(o, i), P.dve)
        P.wait("gpsimd", P.dve, c_ev)
        c_st = P.store(dst_dram_cols(oc), sv)
        c_store[oc] = (c_ev, c_st)
    return c_store[KC - 1]


def build_nc(debug=False, stop_after=None):
    nc = bass.Bass("TRN2", target_bir_lowering=False)
    scr_kind = "ExternalOutput" if debug else "Internal"

    xfm_own = nc.dram_tensor("xfm_own", [D, QH], F32, kind="ExternalInput")
    xfm_oth = nc.dram_tensor("xfm_oth", [D, QH], F32, kind="ExternalInput")
    wts = {}
    for nm in ("wtq", "wtk", "wtv", "wto", "wtf"):
        wts[nm] = nc.dram_tensor(nm, [D, D], F32, kind="ExternalInput")
    disha_d = nc.dram_tensor("disha", [64, D], F32, kind="ExternalInput")
    p0_d = nc.dram_tensor("p0", [128, 64], F32, kind="ExternalInput")
    out_fm = nc.dram_tensor("out_fm", [D, QH], F32, kind="ExternalOutput")
    wbo_scr = nc.dram_tensor("wbo_scr", [D, D], BF16, kind=scr_kind)
    wbf_scr = nc.dram_tensor("wbf_scr", [D, D], BF16, kind=scr_kind)
    k_scr = nc.dram_tensor("k_scr", [D, S], BF16, kind=scr_kind)
    v_scr = nc.dram_tensor("v_scr", [S, D], BF16, kind=scr_kind)
    q_scr = nc.dram_tensor("q_scr", [D, QH], BF16, kind=scr_kind)
    x1_scr = nc.dram_tensor("x1_scr", [D, QH], F32, kind=scr_kind)
    if debug:
        x2_dbg = nc.dram_tensor("x2_dbg", [D, QH], F32, kind="ExternalOutput")
        z_dbg = nc.dram_tensor("z_dbg", [128, QH], F32, kind="ExternalOutput")
        aacc_dbg = nc.dram_tensor("aacc_dbg", [128, NPAIR * QSPAN], F32, kind="ExternalOutput")
        zacc_dbg = nc.dram_tensor("zacc_dbg", [128, NPAIR * QSPAN], F32, kind="ExternalOutput")
        abf_dbg = nc.dram_tensor("abf_dbg", [128, KC * CH], BF16, kind="ExternalOutput")

    from contextlib import ExitStack
    with ExitStack() as _st:
        ec = _st.enter_context
        h_pe = ec(nc.semaphore("s_pe"))
        h_dve = ec(nc.semaphore("s_dve"))
        h_act = ec(nc.semaphore("s_act"))
        h_w = ec(nc.semaphore("s_w"))
        h_g = ec(nc.semaphore("s_g"))
        xc = ec(nc.sbuf_tensor("xc", [128, KC * CH], F32))
        x1c = ec(nc.sbuf_tensor("x1c", [128, KC * CH], F32))
        sqb = ec(nc.sbuf_tensor("sqb", [128, KC * CH], BF16))
        lnb = ec(nc.sbuf_tensor("lnb", [128, KC * CH], BF16))
        tmp = ec(nc.sbuf_tensor("tmp", [128, CH], F32))
        mbc = ec(nc.sbuf_tensor("mbc", [128, CH], F32))
        var = ec(nc.sbuf_tensor("var", [128, CH], F32))
        lnv = ec(nc.sbuf_tensor("lnv", [128, CH], F32))
        rstd = ec(nc.sbuf_tensor("rstd", [128, CH], F32))
        plb = ec(nc.sbuf_tensor("plb", [64, CH], BF16))
        dishab = ec(nc.sbuf_tensor("dishab", [64, D], BF16))
        p0b = ec(nc.sbuf_tensor("p0b", [128, 64], BF16))
        ones128b = ec(nc.sbuf_tensor("ones128b", [128, 128], BF16))
        ones128f = ec(nc.sbuf_tensor("ones128f", [128, 128], F32))
        ones64b = ec(nc.sbuf_tensor("ones64b", [128, 64], BF16))
        pe, dve, act = Sem(h_pe), Sem(h_dve), Sem(h_act),
        W, G = Sem(h_w), Sem(h_g)
        P = Prog(nc, pe, dve, act, W, G)
        bufs = dict(sqb=sqb, lnb=lnb, tmp=tmp, mbc=mbc, var=var, lnv=lnv,
                    rstd=rstd, plb=plb, dishab=dishab, p0b=p0b, ones128b=ones128b)

        with ExitStack() as _stp:
            ecp = _stp.enter_context
            w_f = ecp(nc.sbuf_tensor("w_f", [128, KC * D], BF16))
            xc2 = ecp(nc.sbuf_tensor("xc2", [128, KC * CH], F32))
            x1c2 = ecp(nc.sbuf_tensor("x1c2", [128, KC * CH], F32))
            w_q = ecp(nc.sbuf_tensor("w_q", [128, KC * D], BF16))
            w_k = ecp(nc.sbuf_tensor("w_k", [128, KC * D], BF16))
            w_v = ecp(nc.sbuf_tensor("w_v", [128, KC * D], BF16))
            wstage = ecp(nc.sbuf_tensor("wstage", [128, 4096], BF16))
            wstage_f = wstage[:, :].bitcast(F32)
            qstage = ecp(nc.sbuf_tensor("qstage", [128, 1024], BF16))
            kstage = ecp(nc.sbuf_tensor("kstage", [128, 1024], BF16))
            vstage = ecp(nc.sbuf_tensor("vstage", [128, 2048], BF16))
            Fp = ecp(nc.psum_tensor("Fp", [128, 1024], F32))
            Vp = ecp(nc.psum_tensor("Vp", [128, 1024], F32))
            psums = dict(S1=Fp[:, 0:512], S2=Fp[:, 512:1024], PL=Vp[0:64, 0:512], Fp=Fp)
            bufs["xbb"] = wstage

            P.emit("vector", lambda e, o=ones128b[:, :]: e.memset(o, 1.0))
            P.emit("vector", lambda e, o=ones128f[:, :]: e.memset(o, 1.0))
            c_ones = P.inc("vector", lambda e, o=ones64b[:, :]: e.memset(o, 1.0), P.dve)

            w_map = [(wts["wtq"], w_q, None), (wts["wtk"], w_k, None),
                     (wts["wtv"], w_v, None), (wts["wtf"], w_f, wbf_scr),
                     (wts["wto"], None, wbo_scr)]
            slot = 0
            c_cast_of_slot = {}
            for wd, wsb, scr in w_map:
                wdr = wd.rearrange("(hc p) o -> hc p o", p=128)
                for hc in range(KC):
                    half = slot % 2
                    if slot >= 2:
                        P.wait("sync", P.dve, c_cast_of_slot[slot - 2])
                    c_l = P.load(wstage_f[:, half * 1024:half * 1024 + 1024], wdr[hc])
                    P.wait("vector", P.W, c_l)
                    if wsb is not None:
                        dst = wsb[:, hc * D:(hc + 1) * D]
                    else:
                        P.wait("vector", P.G, G.n)
                        dst = qstage[:, 0:1024]
                    c_cast_of_slot[slot] = P.inc(
                        "vector", lambda e, o=dst,
                        i=wstage_f[:, half * 1024:half * 1024 + 1024]: e.tensor_copy(o, i), P.dve)
                    if scr is not None:
                        P.wait("gpsimd", P.dve, c_cast_of_slot[slot])
                        P.store(scr[hc * 128:(hc + 1) * 128, :], dst)
                    slot += 1
            P.wait("sync", P.dve, c_cast_of_slot[slot - 1])
            c_l = P.load(wstage_f[0:64, 0:1024], disha_d[:, :])
            P.wait("vector", P.W, c_l)
            P.inc("vector", lambda e, o=dishab[:, :], i=wstage_f[0:64, 0:1024]: e.tensor_copy(o, i), P.dve)
            c_l = P.load(wstage_f[:, 1024:1088], p0_d[:, :])
            P.wait("vector", P.W, c_l)
            c_wts = P.inc("vector", lambda e, o=p0b[:, :], i=wstage_f[:, 1024:1088]: e.tensor_copy(o, i), P.dve)

            pv = dict(c_x1=0, c_sq1=0, c_stats2=0, c_var2=c_wts, c_vpe=0, c_vst3=0,
                      c_qst=0, c_kst=0, c_vst=0, c_x1st=0, c_x1_2=0, c_sq1_2=0)
            ci = 0
            for half, src_ in (("own", xfm_own), ("oth", xfm_oth)):
                srcr = src_.rearrange("(hc p) t -> hc p t", p=128)
                for c in range(NCH):
                    xc_c = xc if ci % 2 == 0 else xc2
                    x1c_c = x1c if ci % 2 == 0 else x1c2
                    ci += 1
                    tok_l = c * CH
                    key0 = (0 if half == "own" else QH) + tok_l
                    P.wait("sync", P.dve, pv["c_x1_2"])
                    P.wait("sync", P.act, pv["c_sq1_2"])
                    c_x = 0
                    for hc in range(KC):
                        c_x = P.load(_hc(xc_c, hc), srcr[hc][:, tok_l:tok_l + CH])
                    r1 = emit_ln(P, xc_c, bufs, psums,
                                 {"W": c_x, "dve": pv["c_var2"], "pe": pv["c_stats2"],
                                  "pe_lnb": pv["c_vpe"]})
                    c_sq1 = P.act.n - 1
                    r2 = emit_ffn(P, lnb, xc_c, x1c_c, w_f, bufs, psums, r1["c_lnb"],
                                  c_gfree=pv["c_x1st"])
                    c_x1st = pv["c_x1st"]
                    if half == "own":
                        x1r = x1_scr.rearrange("(hc p) t -> hc p t", p=128)
                        P.wait("gpsimd", P.dve, r2["c_x1"])
                        for hc in range(KC):
                            c_x1st = P.store(x1r[hc][:, tok_l:tok_l + CH], _hc(x1c_c, hc))
                    r3 = emit_ln(P, x1c_c, bufs, psums,
                                 {"dve": r2["c_x1"], "pe": r1["c_stats"], "pe_lnb": 0})
                    c_lnb = r3["c_lnb"]
                    c_prev_evac = r2["c_x1"]
                    c_qst = pv["c_qst"]
                    if half == "own":
                        qr = q_scr.rearrange("(oc p) t -> oc p t", p=128)
                        c_qev, c_qst = emit_proj_fm(P, lnb, w_q, qstage,
                                                    lambda oc, qr=qr, t=tok_l: qr[oc][:, t:t + CH],
                                                    psums, c_lnb, r2["c_x1"], c_gfree=pv["c_qst"])
                    kr = k_scr.rearrange("(oc p) t -> oc p t", p=128)
                    c_kev, c_kst = emit_proj_fm(P, lnb, w_k, kstage,
                                                lambda oc, kr=kr, t=key0: kr[oc][:, t:t + CH],
                                                psums, c_lnb, P.dve.n, c_gfree=pv["c_kst"])
                    P.wait("tensor", P.dve, c_lnb)
                    P.wait("tensor", P.dve, pv["c_vst3"])
                    P.wait("vector", P.G, pv["c_vst"])
                    c_vst = {}
                    c_vpe = 0
                    for tt in range(4):
                        if tt >= 1:
                            P.wait("tensor", P.dve, c_vst[tt - 1][0])
                        c_v = 0
                        for hc in range(KC):
                            for on in range(2):
                                c_v = P.inc("tensor", lambda e, o=Vp[:, on * 512:on * 512 + 512],
                                            l=lnb[:, hc * CH + tt * 128: hc * CH + tt * 128 + 128],
                                            r=w_v[:, hc * D + on * 512: hc * D + on * 512 + 512],
                                            st=(hc == 0), sp=(hc == KC - 1):
                                            e.matmul(o, l, r, start=st, stop=sp), P.pe)
                        c_vpe = c_v
                        sv = vstage[:, (tt % 2) * 1024:(tt % 2) * 1024 + 1024]
                        P.wait("vector", P.pe, c_v)
                        if tt >= 2:
                            P.wait("vector", P.G, c_vst[tt - 2][1])
                        c_ev = P.inc("vector", lambda e, o=sv, i=Vp[:, :]: e.tensor_copy(o, i), P.dve)
                        P.wait("gpsimd", P.dve, c_ev)
                        c_st = P.store(v_scr[key0 + tt * 128: key0 + tt * 128 + 128, :], sv)
                        c_vst[tt] = (c_ev, c_st)
                    pv = dict(c_x1=r2["c_x1"], c_sq1=c_sq1, c_stats2=r3["c_stats"],
                              c_var2=max(r3["c_var"], c_kev), c_vpe=c_vpe, c_vst3=c_vst[3][0],
                              c_qst=c_qst, c_kst=c_kst, c_vst=c_vst[3][1], c_x1st=c_x1st,
                              c_x1_2=pv["c_x1"], c_sq1_2=pv["c_sq1"])
            P.barrier()
            P.run_block()

        if stop_after == "pre":
            P.inc("gpsimd", lambda e: e.dma_start(out_fm[0:128, 0:CH], xc[:, 0:CH]), G, 16)
            P.wait("gpsimd", G, G.n)
            P.run_block()
            return nc

        with ExitStack() as _sta:
            eca = _sta.enter_context
            qspan = eca(nc.sbuf_tensor("qspan", [128, NPAIR * QSPAN], BF16))
            kg = eca(nc.sbuf_tensor("kg", [128, 2 * KC * GK], BF16))
            vg = eca(nc.sbuf_tensor("vg", [128, 2 * GKT * D], BF16))
            wbuf = eca(nc.sbuf_tensor("wbuf", [128, 4 * 2 * QSPAN], BF16))
            attn_sb = eca(nc.sbuf_tensor("attn_sb", [128, NPAIR * QSPAN], F32))
            abf0 = eca(nc.sbuf_tensor("abf0", [128, KC * CH], BF16))
            abf1 = eca(nc.sbuf_tensor("abf1", [128, KC * CH], BF16))
            z_sb = eca(nc.sbuf_tensor("z_sb", [128, NPAIR * QSPAN], F32))
            scores = eca(nc.psum_tensor("scores", [128, 2 * QSPAN], F32))
            av = eca(nc.psum_tensor("av", [128, QSPAN], F32))
            zps = eca(nc.psum_tensor("zps", [128, QSPAN], F32))
            Wo0 = av[:, 0:512]
            Wo1 = av[:, 512:1024]
            tail_psums = dict(S1=scores[:, 1024:1536], S2=scores[:, 1536:2048],
                              PL=zps[0:64, 0:512], Fp=scores[:, 0:1024])
            tail_bufs = dict(bufs)
            tail_bufs["xbb"] = qspan[:, 0:KC * CH]
            kgr = k_scr.rearrange("(hc p) t -> hc p t", p=128)
            qr = q_scr.rearrange("(oc p) t -> oc p t", p=128)
            x1r = x1_scr.rearrange("(hc p) t -> hc p t", p=128)
            outr = out_fm.rearrange("(oc p) t -> oc p t", p=128)

            sigA_cnt, sigB_cnt = {}, {}
            scr_cnt = {}
            avz_cnt = {}
            kv_cnt = {}
            kti = 0
            c_tail_dve = 0
            c_tail_pe = 0
            tail_gates = {"c_stats3": 0, "c_nrm0": 0, "c_wo_last": 0}
            c_q_w = 0
            pend = []
            spill_cnt = {"av": 0, "z": 0}

            def emit_avz(p):
                ps_, pg, ppr, pk4, pkti, pgb = p
                P.wait("tensor", P.act, sigB_cnt[pkti])
                if pk4 == 0:
                    P.wait("tensor", P.dve, spill_cnt["av"])
                wb = wbuf[:, (pkti % 4) * 2 * QSPAN:(pkti % 4 + 1) * 2 * QSPAN]
                vA = vg[:, pgb * GKT * D + pk4 * D + ppr * 128: pgb * GKT * D + pk4 * D + ppr * 128 + 64]
                vB = vg[:, pgb * GKT * D + pk4 * D + ppr * 128 + 64: pgb * GKT * D + pk4 * D + ppr * 128 + 128]
                st0, sp0 = (pk4 == 0), (pk4 == GKT - 1)
                for qs in range(2):
                    wA = wb[:, qs * 512:qs * 512 + 512]
                    wB = wb[:, 1024 + qs * 512:1024 + qs * 512 + 512]
                    P.emit("tensor", lambda e, o=av[0:64, qs * 512:qs * 512 + 512], l=vA, r=wA,
                           st=st0, sp=sp0: e.matmul(o, l, r, start=st, stop=sp, tile_position=(0, 0)))
                    if qs == 0:
                        P.emit("tensor", lambda e, o=av[64:128, qs * 512:qs * 512 + 512], l=vB, r=wB,
                               st=st0, sp=sp0: e.matmul(o, l, r, start=st, stop=sp, tile_position=(0, 64)))
                    else:
                        c_av_mm = P.inc("tensor", lambda e, o=av[64:128, qs * 512:qs * 512 + 512], l=vB, r=wB,
                                        st=st0, sp=sp0: e.matmul(o, l, r, start=st, stop=sp, tile_position=(0, 64)), P.pe)
                if pk4 == 0:
                    P.wait("tensor", P.dve, spill_cnt["z"])
                c_avz = 0
                for qs in range(2):
                    wA = wb[:, qs * 512:qs * 512 + 512]
                    wB = wb[:, 1024 + qs * 512:1024 + qs * 512 + 512]
                    P.emit("tensor", lambda e, o=zps[0:64, qs * 512:qs * 512 + 512], l=ones64b[:, :], r=wA,
                           st=st0, sp=sp0: e.matmul(o, l, r, start=st, stop=sp, tile_position=(0, 0)))
                    c_avz = P.inc("tensor", lambda e, o=zps[64:128, qs * 512:qs * 512 + 512],
                                  l=ones64b[:, :], r=wB,
                                  st=st0, sp=sp0: e.matmul(o, l, r, start=st, stop=sp, tile_position=(0, 64)), P.pe)
                avz_cnt[pkti] = c_avz
                if sp0:
                    P.wait("vector", P.pe, c_av_mm)
                    asl = attn_sb[:, ppr * QSPAN:(ppr + 1) * QSPAN]
                    zsl = z_sb[:, ppr * QSPAN:(ppr + 1) * QSPAN]
                    if pg == 0:
                        spill_cnt["av"] = P.inc("vector", lambda e, o=asl, i=av[:, :]: e.tensor_copy(o, i), P.dve)
                        P.wait("vector", P.pe, c_avz)
                        spill_cnt["z"] = P.inc("vector", lambda e, o=zsl, i=zps[:, :]: e.tensor_copy(o, i), P.dve)
                    else:
                        spill_cnt["av"] = P.inc("vector", lambda e, o=asl, a=asl, b=av[:, :]: e.tensor_add(o, a, b), P.dve)
                        P.wait("vector", P.pe, c_avz)
                        spill_cnt["z"] = P.inc("vector", lambda e, o=zsl, a=zsl, b=zps[:, :]: e.tensor_add(o, a, b), P.dve)
                    if pg == NGRP - 1:
                        P.wait("vector", P.pe, tail_gates["c_wo_last"])
                        P.emit("vector", lambda e, o=zsl, i=zsl: e.tensor_scalar_add(o, i, EPS_W))
                        P.emit("vector", lambda e, o=zsl, i=zsl: e.reciprocal(o, i))
                        P.emit("vector", lambda e, o=_hc(abf0, ppr),
                               a=attn_sb[:, ppr * QSPAN: ppr * QSPAN + 512],
                               b=z_sb[:, ppr * QSPAN: ppr * QSPAN + 512]: e.tensor_mul(o, a, b))
                        tail_gates["c_nrm0"] = P.inc(
                            "vector", lambda e, o=_hc(abf1, ppr),
                            a=attn_sb[:, ppr * QSPAN + 512: ppr * QSPAN + 1024],
                            b=z_sb[:, ppr * QSPAN + 512: ppr * QSPAN + 1024]: e.tensor_mul(o, a, b), P.dve)

            for s in range(NSPAN):
                q0 = s * QSPAN
                if s > 0:
                    P.wait("sync", P.pe, max(scr_cnt[kti - 1], tail_gates["c_stats3"]))
                for pr in range(NPAIR):
                    c_q_w = P.load(qspan[:, pr * QSPAN:(pr + 1) * QSPAN], qr[pr][:, q0:q0 + QSPAN])
                for g_pre in (0, 1):
                    gb = g_pre % 2
                    if s > 0:
                        P.wait("sync", P.pe, c_tail_pe)
                    for hc in range(KC):
                        P.load(kg[:, gb * KC * GK + hc * GK: gb * KC * GK + (hc + 1) * GK],
                               kgr[hc][:, g_pre * GK:(g_pre + 1) * GK])
                    c_kv = 0
                    for tt in range(GKT):
                        c_kv = P.load(vg[:, gb * GKT * D + tt * D: gb * GKT * D + (tt + 1) * D],
                                      v_scr[g_pre * GK + tt * 128: g_pre * GK + (tt + 1) * 128, :])
                    kv_cnt[(s, g_pre)] = c_kv

                c_wt = 0
                for g in range(NGRP):
                    gb = g % 2
                    for pr in range(NPAIR):
                        for k4 in range(GKT):
                            kt = g * GKT + k4
                            P.wait("tensor", P.act, sigA_cnt.get(kti - 1, 0))
                            P.wait("tensor", P.W, kv_cnt[(s, g)])
                            P.wait("tensor", P.W, c_q_w)
                            if g == 0 and pr == 0 and k4 == 0:
                                P.wait("tensor", P.dve, c_tail_dve)
                            kA = kg[0:64, gb * KC * GK + pr * GK + k4 * 128: gb * KC * GK + pr * GK + k4 * 128 + 128]
                            kB = kg[64:128, gb * KC * GK + pr * GK + k4 * 128: gb * KC * GK + pr * GK + k4 * 128 + 128]
                            c_scA = c_scB = 0
                            for qs in range(2):
                                qA = qspan[0:64, pr * QSPAN + qs * 512: pr * QSPAN + qs * 512 + 512]
                                c_scA = P.inc("tensor", lambda e, o=scores[:, qs * 512:qs * 512 + 512],
                                              l=kA, r=qA: e.matmul(o, l, r, start=True, stop=True, tile_position=(0, 0)), P.pe)
                            P.wait("tensor", P.act, sigB_cnt.get(kti - 1, 0))
                            for qs in range(2):
                                qB = qspan[64:128, pr * QSPAN + qs * 512: pr * QSPAN + qs * 512 + 512]
                                c_scB = P.inc("tensor", lambda e, o=scores[:, 1024 + qs * 512:1024 + qs * 512 + 512],
                                              l=kB, r=qB, last=(qs == 1): e.matmul(o, l, r, start=True, stop=True, tile_position=(64, 0)), P.pe)
                            scr_cnt[kti] = c_scB
                            P.wait("scalar", P.pe, c_scA)
                            if kti >= 4:
                                P.wait("scalar", P.pe, avz_cnt[kti - 4])
                            wb = wbuf[:, (kti % 4) * 2 * QSPAN:(kti % 4 + 1) * 2 * QSPAN]
                            sigA_cnt[kti] = P.inc("scalar", lambda e, o=wb[:, 0:1024], i=scores[:, 0:1024]: e.activation(
                                o, i, AF.Sigmoid, scale=SCALE), P.act)
                            P.wait("scalar", P.pe, c_scB)
                            sigB_cnt[kti] = P.inc("scalar", lambda e, o=wb[:, 1024:2048], i=scores[:, 1024:2048]: e.activation(
                                o, i, AF.Sigmoid, scale=SCALE), P.act)
                            if len(pend) >= 2:
                                emit_avz(pend.pop(0))
                            pend.append((s, g, pr, k4, kti, gb))
                            kti += 1
                    while pend:
                        emit_avz(pend.pop(0))
                    if g == NGRP - 2 or g == NGRP - 1:
                        P.wait("sync", P.pe, avz_cnt[kti - 1])
                        h0 = 0 if g == NGRP - 2 else 4
                        for hc in range(h0, h0 + 4):
                            P.load(kg[:, hc * D:(hc + 1) * D], wbo_scr[hc * 128:(hc + 1) * 128, :])
                            c_wt = P.load(vg[:, hc * D:(hc + 1) * D], wbf_scr[hc * 128:(hc + 1) * 128, :])
                    if g + 2 < NGRP:
                        P.wait("sync", P.pe, avz_cnt[kti - 1])
                        for hc in range(KC):
                            P.load(kg[:, gb * KC * GK + hc * GK: gb * KC * GK + (hc + 1) * GK],
                                   kgr[hc][:, (g + 2) * GK:(g + 3) * GK])
                        c_kv = 0
                        for tt in range(GKT):
                            c_kv = P.load(vg[:, gb * GKT * D + tt * D: gb * GKT * D + (tt + 1) * D],
                                          v_scr[(g + 2) * GK + tt * 128: (g + 2) * GK + (tt + 1) * 128, :])
                        kv_cnt[(s, g + 2)] = c_kv

                c_last_sig = sigB_cnt[kti - 1]
                c_spill = P.dve.n
                w_o_t, w_f_t = kg, vg
                if debug and s == 0:
                    P.wait("gpsimd", P.dve, c_spill)
                    P.store(aacc_dbg[:, :], attn_sb[:, :])
                    P.store(zacc_dbg[:, :], z_sb[:, :])
                c_ffn_dve_prev = 0
                c_store_prev = 0
                c_wo_pe = 0
                for sub in range(2):
                    t0 = q0 + sub * 512
                    abf = abf0 if sub == 0 else abf1
                    c_nrm = tail_gates["c_nrm0"]
                    if debug and s == 0 and sub == 0:
                        P.wait("gpsimd", P.dve, c_nrm)
                        P.store(abf_dbg[:, :], sqb[:, :])
                    P.wait("sync", P.dve, c_ffn_dve_prev)
                    c_x1l = 0
                    for hc in range(KC):
                        c_x1l = P.load(_hc(xc, hc), x1r[hc][:, t0:t0 + 512])
                    P.wait("tensor", P.dve, c_nrm)
                    P.wait("tensor", P.act, c_last_sig)
                    P.wait("tensor", P.W, c_wt)
                    P.wait("tensor", P.G, c_store_prev)
                    c_x2 = {}
                    P.wait("vector", P.G, c_store_prev)
                    for oc in range(KC):
                        Fv = Wo0 if oc % 2 == 0 else Wo1
                        if oc >= 2:
                            P.wait("tensor", P.dve, c_x2[oc - 2])
                        c_f = 0
                        for hc in range(KC):
                            c_f = P.inc("tensor", lambda e, o=w_o_t[:, hc * D + oc * 128: hc * D + oc * 128 + 128], r=_hc(abf, hc), oo=Fv,
                                        st=(hc == 0), sp=(hc == KC - 1): e.matmul(oo, o, r, start=st, stop=sp), P.pe)
                        P.wait("vector", P.pe, c_f)
                        P.wait("vector", P.W, c_x1l)
                        c_x2[oc] = P.inc("vector", lambda e, o=_hc(x1c, oc), a=_hc(xc, oc), b=Fv:
                                         e.tensor_add(o, a, b), P.dve)
                    c_wo_pe = P.pe.n
                    if sub == 1:
                        tail_gates["c_wo_last"] = c_wo_pe
                    if debug:
                        x2r = x2_dbg.rearrange("(hc p) t -> hc p t", p=128)
                        P.wait("gpsimd", P.dve, c_x2[KC - 1])
                        for oc in range(KC):
                            P.store(x2r[oc][:, t0:t0 + CH], _hc(x1c, oc))
                        P.store(z_dbg[:, t0:t0 + CH], z_sb[:, 0 * QSPAN + sub * 512: 0 * QSPAN + sub * 512 + 512])
                    P.wait("scalar", P.pe, c_wo_pe)
                    c_xbb_pe = scr_cnt[kti - 1] if sub == 0 else c_stats3
                    r1 = emit_ln(P, x1c, tail_bufs, tail_psums,
                                 {"dve": c_x2[KC - 1], "pe": c_xbb_pe})
                    c_stats3 = r1["c_stats"]
                    r2 = emit_ffn(P, lnb, x1c, x1c, w_f_t, tail_bufs, tail_psums, r1["c_lnb"],
                                  c_gfree=c_store_prev)
                    c_ffn_dve_prev = r2["c_x1"]
                    P.wait("gpsimd", P.dve, r2["c_x1"])
                    c_store_prev = 0
                    for oc in range(KC):
                        c_store_prev = P.store(outr[oc][:, t0:t0 + 512], _hc(x1c, oc))
                c_tail_dve = P.dve.n
                c_tail_pe = P.pe.n
                tail_gates["c_stats3"] = c_stats3

            P.wait("gpsimd", P.G, P.G.n)
            P.run_block()
    return nc


def kernel(L0_attn_x, L0_attn_Wq, L0_attn_Wk, L0_attn_Wv, L0_attn_Wo,
           L0_ffn_weight, L0_ffn_disha):
    x = np.asarray(L0_attn_x, dtype=np.float32)
    wq = np.ascontiguousarray(np.asarray(L0_attn_Wq, dtype=np.float32).T)
    wk = np.ascontiguousarray(np.asarray(L0_attn_Wk, dtype=np.float32).T)
    wv = np.ascontiguousarray(np.asarray(L0_attn_Wv, dtype=np.float32).T)
    wo = np.ascontiguousarray(np.asarray(L0_attn_Wo, dtype=np.float32).T)
    wf = np.ascontiguousarray(np.asarray(L0_ffn_weight, dtype=np.float32).T)
    disha = np.ascontiguousarray(np.asarray(L0_ffn_disha, dtype=np.float32))
    p0 = np.ascontiguousarray(np.tile(np.eye(64, dtype=np.float32), (2, 1)))

    nc = build_nc()
    in_maps = []
    for core in range(8):
        b, h = core // 2, core % 2
        xo = np.ascontiguousarray(x[b, h * QH:(h + 1) * QH, :].T)
        xt = np.ascontiguousarray(x[b, (1 - h) * QH:(2 - h) * QH, :].T)
        in_maps.append({"xfm_own": xo, "xfm_oth": xt, "wtq": wq, "wtk": wk,
                        "wtv": wv, "wto": wo, "wtf": wf, "disha": disha, "p0": p0})
    res = run_bass_kernel_spmd(nc, in_maps, core_ids=list(range(8)))
    out = np.empty((B, S, D), dtype=np.float32)
    for core in range(8):
        b, h = core // 2, core % 2
        out[b, h * QH:(h + 1) * QH, :] = res.results[core]["out_fm"].T
    return out


# version 26
# speedup vs baseline: 1.2405x; 1.0389x over previous
import numpy as np
import concourse.bass as bass
import concourse.mybir as mybir
from concourse.bass_utils import run_bass_kernel_spmd

F32 = mybir.dt.float32
BF16 = mybir.dt.bfloat16
AF = mybir.ActivationFunctionType
OP = mybir.AluOpType

B, S, D = 4, 8192, 1024
H, HD = 16, 64
QH = S // 2
KC = D // 128
CH = 512
NCH = QH // CH
SCALE = HD ** -0.5
EPS_LN = 1e-5
EPS_W = 1e-6
QSPAN = 1024
NSPAN = QH // QSPAN
GKT = 4
GK = GKT * 128
NGRP = S // GK
NPAIR = KC
INV_D = 1.0 / D


class Sem:
    def __init__(self, handle):
        self.h = handle
        self.n = 0


class Prog:
    ENGS = ("tensor", "vector", "scalar", "sync", "gpsimd")

    def __init__(self, nc, pe, dve, act, W, G):
        self.nc = nc
        self.pe, self.dve, self.act, self.W, self.G = pe, dve, act, W, G
        self.ops = {e: [] for e in self.ENGS}

    def emit(self, eng, thunk, inc=None):
        self.ops[eng].append((thunk, inc))

    def inc(self, eng, thunk, sem, k=1):
        sem.n += k
        self.emit(eng, thunk, inc=(sem, k))
        return sem.n

    def wait(self, eng, sem, cnt):
        if cnt <= 0:
            return
        self.emit(eng, lambda e, s=sem.h, c=cnt: e.wait_ge(s, c))

    def load(self, dst, src):
        return self.inc("sync", lambda e, d=dst, s=src: e.dma_start(d, s), self.W, 16)

    def store(self, dst, src):
        return self.inc("gpsimd", lambda e, d=dst, s=src: e.dma_start(d, s), self.G, 16)

    def barrier(self):
        marks = [(self.pe, self.pe.n), (self.dve, self.dve.n), (self.act, self.act.n),
                 (self.W, self.W.n), (self.G, self.G.n)]
        for eng in self.ENGS:
            for sem, c in marks:
                self.wait(eng, sem, c)

    def run_block(self):
        with self.nc.Block() as block:
            for name in self.ENGS:
                opl = self.ops[name]

                def body(eng, opl=opl):
                    for thunk, inc in opl:
                        ins = thunk(eng)
                        if inc is not None:
                            ins.then_inc(inc[0].h, inc[1])

                getattr(block, name)(body)
        self.ops = {e: [] for e in self.ENGS}


def _hc(buf, hc, n=CH):
    return buf[:, hc * n:(hc + 1) * n]


def emit_ln(P, xin, bufs, psums, c_xin_ready):
    sqb, lnb, xbb = bufs["sqb"], bufs["lnb"], bufs["xbb"]
    tmp, mbc, var, lnv, rstd = bufs["tmp"], bufs["mbc"], bufs["var"], bufs["lnv"], bufs["rstd"]
    ones = bufs["ones128b"]
    S1, S2 = psums["S1"], psums["S2"]

    P.wait("vector", P.dve, c_xin_ready.get("dve", 0))
    P.wait("vector", P.pe, c_xin_ready.get("pe", 0))
    P.wait("vector", P.W, c_xin_ready.get("W", 0))
    P.emit("vector", lambda e, o=xbb[:, :], i=xin[:, :]: e.tensor_copy(o, i))
    c_sq = P.inc("vector", lambda e, o=sqb[:, :], a=xbb[:, :], b=xbb[:, :]: e.tensor_mul(o, a, b), P.dve)
    P.wait("tensor", P.dve, c_sq)
    for hc in range(KC):
        P.emit("tensor", lambda e, o=S1[:, :], l=ones[:, :], r=_hc(xbb, hc),
               st=(hc == 0), sp=(hc == KC - 1): e.matmul(o, l, r, start=st, stop=sp))
    c_stats = 0
    for hc in range(KC):
        c_stats = P.inc("tensor", lambda e, o=S2[:, :], l=ones[:, :], r=_hc(sqb, hc),
                        st=(hc == 0), sp=(hc == KC - 1): e.matmul(o, l, r, start=st, stop=sp), P.pe)
    P.wait("vector", P.pe, c_stats)
    P.emit("vector", lambda e, o=mbc[:, :], i=S1[:, :]: e.tensor_scalar_mul(o, i, INV_D))
    P.emit("vector", lambda e, o=tmp[:, :], a=mbc[:, :], b=mbc[:, :]: e.tensor_mul(o, a, b))
    P.emit("vector", lambda e, o=var[:, :], i0=S2[:, :], t=tmp[:, :]: e.scalar_tensor_tensor(
        o, i0, INV_D, t, OP.mult, OP.subtract))
    c_var = P.inc("vector", lambda e, o=var[:, :], i=var[:, :]: e.tensor_scalar_add(o, i, EPS_LN), P.dve)
    P.wait("scalar", P.dve, c_var)
    P.emit("scalar", lambda e, o=lnv[:, :], i=var[:, :]: e.activation(o, i, AF.Ln))
    c_rstd = P.inc("scalar", lambda e, o=rstd[:, :], i=lnv[:, :]: e.activation(o, i, AF.Exp, scale=-0.5), P.act)
    P.wait("vector", P.act, c_rstd)
    P.wait("vector", P.pe, c_xin_ready.get("pe_lnb", 0))
    c_lnb = 0
    for hc in range(KC):
        P.emit("vector", lambda e, o=tmp[:, :], a=_hc(xin, hc), b=mbc[:, :]: e.tensor_sub(o, a, b))
        c_lnb = P.inc("vector", lambda e, o=_hc(lnb, hc), a=tmp[:, :], b=rstd[:, :]: e.tensor_mul(o, a, b), P.dve)
    return {"c_lnb": c_lnb, "c_stats": c_stats, "c_var": c_var}


def emit_ffn(P, lnb, xin, xout, w_sb, bufs, psums, c_lnb, c_gfree=0):
    plb, dishab, p0b = bufs["plb"], bufs["dishab"], bufs["p0b"]
    PL, Fp = psums["PL"], psums["Fp"]
    P.wait("vector", P.G, c_gfree)
    P.wait("tensor", P.dve, c_lnb)
    c_pl = 0
    for hc in range(KC):
        c_pl = P.inc("tensor", lambda e, o=PL[:, :], l=p0b[:, :], r=_hc(lnb, hc),
                     st=(hc == 0), sp=(hc == KC - 1): e.matmul(o, l, r, start=st, stop=sp), P.pe)
    P.wait("vector", P.pe, c_pl)
    c_plb = P.inc("vector", lambda e, o=plb[:, :], i=PL[:, :]: e.tensor_copy(o, i), P.dve)
    P.wait("tensor", P.dve, c_plb)
    c_x1 = {}
    for oc in range(KC):
        Fv = Fp[:, (oc % 2) * 512:(oc % 2) * 512 + 512]
        if oc >= 2:
            P.wait("tensor", P.dve, c_x1[oc - 2])
        for hc in range(KC):
            P.emit("tensor", lambda e, o=Fv, l=w_sb[:, hc * D + oc * 128: hc * D + oc * 128 + 128],
                   r=_hc(lnb, hc), st=(hc == 0): e.matmul(o, l, r, start=st, stop=False))
        c_f = P.inc("tensor", lambda e, o=Fv, l=dishab[0:64, oc * 128:oc * 128 + 128],
                    r=plb[:, :]: e.matmul(o, l, r, start=False, stop=True), P.pe)
        P.wait("vector", P.pe, c_f)
        c_x1[oc] = P.inc("vector", lambda e, o=_hc(xout, oc), i0=Fv, i1=_hc(xin, oc): e.scalar_tensor_tensor(
            o, i0, 0.5, i1, OP.mult, OP.add), P.dve)
    return {"c_x1": c_x1[KC - 1], "c_pe_end": P.pe.n}


def emit_proj_fm(P, lnb, w_sb, stage, dst_dram_cols, P_, c_lnb, c_fp_free=0, c_gfree=0):
    Fp = P_["Fp"]
    P.wait("tensor", P.dve, c_lnb)
    P.wait("tensor", P.dve, c_fp_free)
    P.wait("vector", P.G, c_gfree)
    c_store = {}
    for oc in range(KC):
        Fv = Fp[:, (oc % 2) * 512:(oc % 2) * 512 + 512]
        if oc >= 2:
            P.wait("tensor", P.dve, c_store[oc - 2][0])
        c_f = 0
        for hc in range(KC):
            c_f = P.inc("tensor", lambda e, o=Fv, l=w_sb[:, hc * D + oc * 128: hc * D + oc * 128 + 128],
                        r=_hc(lnb, hc), st=(hc == 0), sp=(hc == KC - 1): e.matmul(o, l, r, start=st, stop=sp), P.pe)
        sv = stage[:, (oc % 2) * 512:(oc % 2) * 512 + 512]
        P.wait("vector", P.pe, c_f)
        if oc >= 2:
            P.wait("vector", P.G, c_store[oc - 2][1])
        c_ev = P.inc("vector", lambda e, o=sv, i=Fv: e.tensor_copy(o, i), P.dve)
        P.wait("gpsimd", P.dve, c_ev)
        c_st = P.store(dst_dram_cols(oc), sv)
        c_store[oc] = (c_ev, c_st)
    return c_store[KC - 1]


def build_nc(debug=False, stop_after=None):
    nc = bass.Bass("TRN2", target_bir_lowering=False)
    scr_kind = "ExternalOutput" if debug else "Internal"

    xfm_own = nc.dram_tensor("xfm_own", [D, QH], F32, kind="ExternalInput")
    xfm_oth = nc.dram_tensor("xfm_oth", [D, QH], F32, kind="ExternalInput")
    wts = {}
    for nm in ("wtq", "wtk", "wtv", "wto", "wtf"):
        wts[nm] = nc.dram_tensor(nm, [D, D], F32, kind="ExternalInput")
    disha_d = nc.dram_tensor("disha", [64, D], F32, kind="ExternalInput")
    p0_d = nc.dram_tensor("p0", [128, 64], F32, kind="ExternalInput")
    out_fm = nc.dram_tensor("out_fm", [D, QH], F32, kind="ExternalOutput")
    wbo_scr = nc.dram_tensor("wbo_scr", [D, D], BF16, kind=scr_kind)
    wbf_scr = nc.dram_tensor("wbf_scr", [D, D], BF16, kind=scr_kind)
    k_scr = nc.dram_tensor("k_scr", [D, S], BF16, kind=scr_kind)
    v_scr = nc.dram_tensor("v_scr", [S, D], BF16, kind=scr_kind)
    q_scr = nc.dram_tensor("q_scr", [D, QH], BF16, kind=scr_kind)
    x1_scr = nc.dram_tensor("x1_scr", [D, QH], F32, kind=scr_kind)
    if debug:
        x2_dbg = nc.dram_tensor("x2_dbg", [D, QH], F32, kind="ExternalOutput")
        z_dbg = nc.dram_tensor("z_dbg", [128, QH], F32, kind="ExternalOutput")
        aacc_dbg = nc.dram_tensor("aacc_dbg", [128, NPAIR * QSPAN], F32, kind="ExternalOutput")
        zacc_dbg = nc.dram_tensor("zacc_dbg", [128, NPAIR * QSPAN], F32, kind="ExternalOutput")
        abf_dbg = nc.dram_tensor("abf_dbg", [128, KC * CH], BF16, kind="ExternalOutput")

    from contextlib import ExitStack
    with ExitStack() as _st:
        ec = _st.enter_context
        h_pe = ec(nc.semaphore("s_pe"))
        h_dve = ec(nc.semaphore("s_dve"))
        h_act = ec(nc.semaphore("s_act"))
        h_w = ec(nc.semaphore("s_w"))
        h_g = ec(nc.semaphore("s_g"))
        xc = ec(nc.sbuf_tensor("xc", [128, KC * CH], F32))
        x1c = ec(nc.sbuf_tensor("x1c", [128, KC * CH], F32))
        sqb = ec(nc.sbuf_tensor("sqb", [128, KC * CH], BF16))
        lnb = ec(nc.sbuf_tensor("lnb", [128, KC * CH], BF16))
        tmp = ec(nc.sbuf_tensor("tmp", [128, CH], F32))
        mbc = ec(nc.sbuf_tensor("mbc", [128, CH], F32))
        var = ec(nc.sbuf_tensor("var", [128, CH], F32))
        lnv = ec(nc.sbuf_tensor("lnv", [128, CH], F32))
        rstd = ec(nc.sbuf_tensor("rstd", [128, CH], F32))
        plb = ec(nc.sbuf_tensor("plb", [64, CH], BF16))
        dishab = ec(nc.sbuf_tensor("dishab", [64, D], BF16))
        p0b = ec(nc.sbuf_tensor("p0b", [128, 64], BF16))
        ones128b = ec(nc.sbuf_tensor("ones128b", [128, 128], BF16))
        ones128f = ec(nc.sbuf_tensor("ones128f", [128, 128], F32))
        ones64b = ec(nc.sbuf_tensor("ones64b", [128, 64], BF16))
        pe, dve, act = Sem(h_pe), Sem(h_dve), Sem(h_act),
        W, G = Sem(h_w), Sem(h_g)
        P = Prog(nc, pe, dve, act, W, G)
        bufs = dict(sqb=sqb, lnb=lnb, tmp=tmp, mbc=mbc, var=var, lnv=lnv,
                    rstd=rstd, plb=plb, dishab=dishab, p0b=p0b, ones128b=ones128b)

        with ExitStack() as _stp:
            ecp = _stp.enter_context
            w_f = ecp(nc.sbuf_tensor("w_f", [128, KC * D], BF16))
            xc2 = ecp(nc.sbuf_tensor("xc2", [128, KC * CH], F32))
            x1c2 = ecp(nc.sbuf_tensor("x1c2", [128, KC * CH], F32))
            w_q = ecp(nc.sbuf_tensor("w_q", [128, KC * D], BF16))
            w_k = ecp(nc.sbuf_tensor("w_k", [128, KC * D], BF16))
            w_v = ecp(nc.sbuf_tensor("w_v", [128, KC * D], BF16))
            wstage = ecp(nc.sbuf_tensor("wstage", [128, 4096], BF16))
            wstage_f = wstage[:, :].bitcast(F32)
            qstage = ecp(nc.sbuf_tensor("qstage", [128, 1024], BF16))
            kstage = ecp(nc.sbuf_tensor("kstage", [128, 1024], BF16))
            vstage = ecp(nc.sbuf_tensor("vstage", [128, 2048], BF16))
            Fp = ecp(nc.psum_tensor("Fp", [128, 1024], F32))
            Vp = ecp(nc.psum_tensor("Vp", [128, 1024], F32))
            psums = dict(S1=Fp[:, 0:512], S2=Fp[:, 512:1024], PL=Vp[0:64, 0:512], Fp=Fp)
            bufs["xbb"] = wstage

            P.emit("vector", lambda e, o=ones128b[:, :]: e.memset(o, 1.0))
            P.emit("vector", lambda e, o=ones128f[:, :]: e.memset(o, 1.0))
            c_ones = P.inc("vector", lambda e, o=ones64b[:, :]: e.memset(o, 1.0), P.dve)

            w_map = [(wts["wtq"], w_q, None), (wts["wtk"], w_k, None),
                     (wts["wtv"], w_v, None), (wts["wtf"], w_f, wbf_scr),
                     (wts["wto"], None, wbo_scr)]
            slot = 0
            c_cast_of_slot = {}
            for wd, wsb, scr in w_map:
                wdr = wd.rearrange("(hc p) o -> hc p o", p=128)
                for hc in range(KC):
                    half = slot % 2
                    if slot >= 2:
                        P.wait("sync", P.dve, c_cast_of_slot[slot - 2])
                    c_l = P.load(wstage_f[:, half * 1024:half * 1024 + 1024], wdr[hc])
                    P.wait("vector", P.W, c_l)
                    if wsb is not None:
                        dst = wsb[:, hc * D:(hc + 1) * D]
                    else:
                        P.wait("vector", P.G, G.n)
                        dst = qstage[:, 0:1024]
                    c_cast_of_slot[slot] = P.inc(
                        "vector", lambda e, o=dst,
                        i=wstage_f[:, half * 1024:half * 1024 + 1024]: e.tensor_copy(o, i), P.dve)
                    if scr is not None:
                        P.wait("gpsimd", P.dve, c_cast_of_slot[slot])
                        P.store(scr[hc * 128:(hc + 1) * 128, :], dst)
                    slot += 1
            P.wait("sync", P.dve, c_cast_of_slot[slot - 1])
            c_l = P.load(wstage_f[0:64, 0:1024], disha_d[:, :])
            P.wait("vector", P.W, c_l)
            P.inc("vector", lambda e, o=dishab[:, :], i=wstage_f[0:64, 0:1024]: e.tensor_copy(o, i), P.dve)
            c_l = P.load(wstage_f[:, 1024:1088], p0_d[:, :])
            P.wait("vector", P.W, c_l)
            c_wts = P.inc("vector", lambda e, o=p0b[:, :], i=wstage_f[:, 1024:1088]: e.tensor_copy(o, i), P.dve)

            pv = dict(c_x1=0, c_sq1=0, c_stats2=0, c_var2=c_wts, c_vpe=0, c_vst3=0,
                      c_qst=0, c_kst=0, c_vst=0, c_x1st=0, c_x1_2=0, c_sq1_2=0)
            ci = 0
            for half, src_ in (("own", xfm_own), ("oth", xfm_oth)):
                srcr = src_.rearrange("(hc p) t -> hc p t", p=128)
                for c in range(NCH):
                    xc_c = xc if ci % 2 == 0 else xc2
                    x1c_c = x1c if ci % 2 == 0 else x1c2
                    ci += 1
                    tok_l = c * CH
                    key0 = (0 if half == "own" else QH) + tok_l
                    P.wait("sync", P.dve, pv["c_x1_2"])
                    P.wait("sync", P.act, pv["c_sq1_2"])
                    c_x = 0
                    for hc in range(KC):
                        c_x = P.load(_hc(xc_c, hc), srcr[hc][:, tok_l:tok_l + CH])
                    r1 = emit_ln(P, xc_c, bufs, psums,
                                 {"W": c_x, "dve": pv["c_var2"], "pe": pv["c_stats2"],
                                  "pe_lnb": pv["c_vpe"]})
                    c_sq1 = P.act.n - 1
                    r2 = emit_ffn(P, lnb, xc_c, x1c_c, w_f, bufs, psums, r1["c_lnb"],
                                  c_gfree=pv["c_x1st"])
                    c_x1st = pv["c_x1st"]
                    if half == "own":
                        x1r = x1_scr.rearrange("(hc p) t -> hc p t", p=128)
                        P.wait("gpsimd", P.dve, r2["c_x1"])
                        for hc in range(KC):
                            c_x1st = P.store(x1r[hc][:, tok_l:tok_l + CH], _hc(x1c_c, hc))
                    r3 = emit_ln(P, x1c_c, bufs, psums,
                                 {"dve": r2["c_x1"], "pe": r1["c_stats"], "pe_lnb": 0})
                    c_lnb = r3["c_lnb"]
                    c_prev_evac = r2["c_x1"]
                    c_qst = pv["c_qst"]
                    if half == "own":
                        qr = q_scr.rearrange("(oc p) t -> oc p t", p=128)
                        c_qev, c_qst = emit_proj_fm(P, lnb, w_q, qstage,
                                                    lambda oc, qr=qr, t=tok_l: qr[oc][:, t:t + CH],
                                                    psums, c_lnb, r2["c_x1"], c_gfree=pv["c_qst"])
                    kr = k_scr.rearrange("(oc p) t -> oc p t", p=128)
                    c_kev, c_kst = emit_proj_fm(P, lnb, w_k, kstage,
                                                lambda oc, kr=kr, t=key0: kr[oc][:, t:t + CH],
                                                psums, c_lnb, P.dve.n, c_gfree=pv["c_kst"])
                    P.wait("tensor", P.dve, c_lnb)
                    P.wait("tensor", P.dve, pv["c_vst3"])
                    P.wait("vector", P.G, pv["c_vst"])
                    c_vst = {}
                    c_vpe = 0
                    for tt in range(4):
                        if tt >= 1:
                            P.wait("tensor", P.dve, c_vst[tt - 1][0])
                        c_v = 0
                        for hc in range(KC):
                            for on in range(2):
                                c_v = P.inc("tensor", lambda e, o=Vp[:, on * 512:on * 512 + 512],
                                            l=lnb[:, hc * CH + tt * 128: hc * CH + tt * 128 + 128],
                                            r=w_v[:, hc * D + on * 512: hc * D + on * 512 + 512],
                                            st=(hc == 0), sp=(hc == KC - 1):
                                            e.matmul(o, l, r, start=st, stop=sp), P.pe)
                        c_vpe = c_v
                        sv = vstage[:, (tt % 2) * 1024:(tt % 2) * 1024 + 1024]
                        P.wait("vector", P.pe, c_v)
                        if tt >= 2:
                            P.wait("vector", P.G, c_vst[tt - 2][1])
                        c_ev = P.inc("vector", lambda e, o=sv, i=Vp[:, :]: e.tensor_copy(o, i), P.dve)
                        P.wait("gpsimd", P.dve, c_ev)
                        c_st = P.store(v_scr[key0 + tt * 128: key0 + tt * 128 + 128, :], sv)
                        c_vst[tt] = (c_ev, c_st)
                    pv = dict(c_x1=r2["c_x1"], c_sq1=c_sq1, c_stats2=r3["c_stats"],
                              c_var2=max(r3["c_var"], c_kev), c_vpe=c_vpe, c_vst3=c_vst[3][0],
                              c_qst=c_qst, c_kst=c_kst, c_vst=c_vst[3][1], c_x1st=c_x1st,
                              c_x1_2=pv["c_x1"], c_sq1_2=pv["c_sq1"])
            P.barrier()
            P.run_block()

        if stop_after == "pre":
            P.inc("gpsimd", lambda e: e.dma_start(out_fm[0:128, 0:CH], xc[:, 0:CH]), G, 16)
            P.wait("gpsimd", G, G.n)
            P.run_block()
            return nc

        with ExitStack() as _sta:
            eca = _sta.enter_context
            qspan = eca(nc.sbuf_tensor("qspan", [128, NPAIR * QSPAN], BF16))
            kg = eca(nc.sbuf_tensor("kg", [128, 2 * KC * GK], BF16))
            vg = eca(nc.sbuf_tensor("vg", [128, 2 * GKT * D], BF16))
            wbuf = eca(nc.sbuf_tensor("wbuf", [128, 4 * 2 * QSPAN], BF16))
            attn_sb = eca(nc.sbuf_tensor("attn_sb", [128, NPAIR * QSPAN], F32))
            abf0 = eca(nc.sbuf_tensor("abf0", [128, KC * CH], BF16))
            abf1 = eca(nc.sbuf_tensor("abf1", [128, KC * CH], BF16))
            z_sb = eca(nc.sbuf_tensor("z_sb", [128, NPAIR * QSPAN], F32))
            scores = eca(nc.psum_tensor("scores", [128, 2 * QSPAN], F32))
            av = eca(nc.psum_tensor("av", [128, QSPAN], F32))
            zps = eca(nc.psum_tensor("zps", [128, QSPAN], F32))
            Wo0 = av[:, 0:512]
            Wo1 = av[:, 512:1024]
            tail_psums = dict(S1=scores[:, 1024:1536], S2=scores[:, 1536:2048],
                              PL=zps[0:64, 0:512], Fp=scores[:, 0:1024])
            tail_bufs = dict(bufs)
            tail_bufs["xbb"] = qspan[:, 0:KC * CH]
            kgr = k_scr.rearrange("(hc p) t -> hc p t", p=128)
            qr = q_scr.rearrange("(oc p) t -> oc p t", p=128)
            x1r = x1_scr.rearrange("(hc p) t -> hc p t", p=128)
            outr = out_fm.rearrange("(oc p) t -> oc p t", p=128)

            sigA_cnt, sigB_cnt = {}, {}
            scr_cnt = {}
            avz_cnt = {}
            kv_cnt = {}
            kti = 0
            c_tail_dve = 0
            c_tail_pe = 0
            tail_gates = {"c_stats3": 0, "c_nrm0": 0, "c_wo_last": 0, "c_out_st": 0}
            c_q_w = 0
            pend = []
            spill_cnt = {"av": 0, "z": 0}

            def emit_avz(p):
                ps_, pg, ppr, pk4, pkti, pgb = p
                P.wait("tensor", P.act, sigB_cnt[pkti])
                if pk4 == 0:
                    P.wait("tensor", P.dve, spill_cnt["av"])
                wb = wbuf[:, (pkti % 4) * 2 * QSPAN:(pkti % 4 + 1) * 2 * QSPAN]
                vA = vg[:, pgb * GKT * D + pk4 * D + ppr * 128: pgb * GKT * D + pk4 * D + ppr * 128 + 64]
                vB = vg[:, pgb * GKT * D + pk4 * D + ppr * 128 + 64: pgb * GKT * D + pk4 * D + ppr * 128 + 128]
                st0, sp0 = (pk4 == 0), (pk4 == GKT - 1)
                for qs in range(2):
                    wA = wb[:, qs * 512:qs * 512 + 512]
                    wB = wb[:, 1024 + qs * 512:1024 + qs * 512 + 512]
                    P.emit("tensor", lambda e, o=av[0:64, qs * 512:qs * 512 + 512], l=vA, r=wA,
                           st=st0, sp=sp0: e.matmul(o, l, r, start=st, stop=sp, tile_position=(0, 0)))
                    if qs == 0:
                        P.emit("tensor", lambda e, o=av[64:128, qs * 512:qs * 512 + 512], l=vB, r=wB,
                               st=st0, sp=sp0: e.matmul(o, l, r, start=st, stop=sp, tile_position=(0, 64)))
                    else:
                        c_av_mm = P.inc("tensor", lambda e, o=av[64:128, qs * 512:qs * 512 + 512], l=vB, r=wB,
                                        st=st0, sp=sp0: e.matmul(o, l, r, start=st, stop=sp, tile_position=(0, 64)), P.pe)
                if pk4 == 0:
                    P.wait("tensor", P.dve, spill_cnt["z"])
                c_avz = 0
                for qs in range(2):
                    wA = wb[:, qs * 512:qs * 512 + 512]
                    wB = wb[:, 1024 + qs * 512:1024 + qs * 512 + 512]
                    P.emit("tensor", lambda e, o=zps[0:64, qs * 512:qs * 512 + 512], l=ones64b[:, :], r=wA,
                           st=st0, sp=sp0: e.matmul(o, l, r, start=st, stop=sp, tile_position=(0, 0)))
                    c_avz = P.inc("tensor", lambda e, o=zps[64:128, qs * 512:qs * 512 + 512],
                                  l=ones64b[:, :], r=wB,
                                  st=st0, sp=sp0: e.matmul(o, l, r, start=st, stop=sp, tile_position=(0, 64)), P.pe)
                avz_cnt[pkti] = c_avz
                if sp0:
                    P.wait("vector", P.pe, c_av_mm)
                    asl = attn_sb[:, ppr * QSPAN:(ppr + 1) * QSPAN]
                    zsl = z_sb[:, ppr * QSPAN:(ppr + 1) * QSPAN]
                    if pg == 0:
                        spill_cnt["av"] = P.inc("vector", lambda e, o=asl, i=av[:, :]: e.tensor_copy(o, i), P.dve)
                        P.wait("vector", P.pe, c_avz)
                        spill_cnt["z"] = P.inc("vector", lambda e, o=zsl, i=zps[:, :]: e.tensor_copy(o, i), P.dve)
                    else:
                        spill_cnt["av"] = P.inc("vector", lambda e, o=asl, a=asl, b=av[:, :]: e.tensor_add(o, a, b), P.dve)
                        P.wait("vector", P.pe, c_avz)
                        spill_cnt["z"] = P.inc("vector", lambda e, o=zsl, a=zsl, b=zps[:, :]: e.tensor_add(o, a, b), P.dve)
                    if pg == NGRP - 1:
                        P.wait("vector", P.pe, tail_gates["c_wo_last"])
                        P.emit("vector", lambda e, o=zsl, i=zsl: e.tensor_scalar_add(o, i, EPS_W))
                        P.emit("vector", lambda e, o=zsl, i=zsl: e.reciprocal(o, i))
                        P.emit("vector", lambda e, o=_hc(abf0, ppr),
                               a=attn_sb[:, ppr * QSPAN: ppr * QSPAN + 512],
                               b=z_sb[:, ppr * QSPAN: ppr * QSPAN + 512]: e.tensor_mul(o, a, b))
                        tail_gates["c_nrm0"] = P.inc(
                            "vector", lambda e, o=_hc(abf1, ppr),
                            a=attn_sb[:, ppr * QSPAN + 512: ppr * QSPAN + 1024],
                            b=z_sb[:, ppr * QSPAN + 512: ppr * QSPAN + 1024]: e.tensor_mul(o, a, b), P.dve)

            for s in range(NSPAN):
                q0 = s * QSPAN
                if s > 0:
                    P.wait("sync", P.pe, max(scr_cnt[kti - 1], tail_gates["c_stats3"]))
                for pr in range(NPAIR):
                    c_q_w = P.load(qspan[:, pr * QSPAN:(pr + 1) * QSPAN], qr[pr][:, q0:q0 + QSPAN])
                for g_pre in (0, 1):
                    gb = g_pre % 2
                    if s > 0:
                        P.wait("sync", P.pe, c_tail_pe)
                    for hc in range(KC):
                        P.load(kg[:, gb * KC * GK + hc * GK: gb * KC * GK + (hc + 1) * GK],
                               kgr[hc][:, g_pre * GK:(g_pre + 1) * GK])
                    c_kv = 0
                    for tt in range(GKT):
                        c_kv = P.load(vg[:, gb * GKT * D + tt * D: gb * GKT * D + (tt + 1) * D],
                                      v_scr[g_pre * GK + tt * 128: g_pre * GK + (tt + 1) * 128, :])
                    kv_cnt[(s, g_pre)] = c_kv

                c_wt = 0
                grp_last = {}
                for g in range(NGRP):
                    gb = g % 2
                    for pr in range(NPAIR):
                        if pr == 1:
                            if g >= 1 and g + 1 < NGRP:
                                gpb = (g + 1) % 2
                                P.wait("sync", P.pe, avz_cnt[grp_last[g - 1]])
                                for hc in range(KC):
                                    P.load(kg[:, gpb * KC * GK + hc * GK: gpb * KC * GK + (hc + 1) * GK],
                                           kgr[hc][:, (g + 1) * GK:(g + 2) * GK])
                                c_kv = 0
                                for tt in range(GKT):
                                    c_kv = P.load(vg[:, gpb * GKT * D + tt * D: gpb * GKT * D + (tt + 1) * D],
                                                  v_scr[(g + 1) * GK + tt * 128: (g + 1) * GK + (tt + 1) * 128, :])
                                kv_cnt[(s, g + 1)] = c_kv
                            if g == NGRP - 1:
                                P.wait("sync", P.pe, avz_cnt[grp_last[g - 1]])
                                for hc in range(0, 4):
                                    P.load(kg[:, hc * D:(hc + 1) * D], wbo_scr[hc * 128:(hc + 1) * 128, :])
                                    c_wt = P.load(vg[:, hc * D:(hc + 1) * D], wbf_scr[hc * 128:(hc + 1) * 128, :])
                        for k4 in range(GKT):
                            kt = g * GKT + k4
                            P.wait("tensor", P.act, sigA_cnt.get(kti - 1, 0))
                            P.wait("tensor", P.W, kv_cnt[(s, g)])
                            P.wait("tensor", P.W, c_q_w)
                            if g == 0 and pr == 0 and k4 == 0:
                                P.wait("tensor", P.dve, c_tail_dve)
                            kA = kg[0:64, gb * KC * GK + pr * GK + k4 * 128: gb * KC * GK + pr * GK + k4 * 128 + 128]
                            kB = kg[64:128, gb * KC * GK + pr * GK + k4 * 128: gb * KC * GK + pr * GK + k4 * 128 + 128]
                            c_scA = c_scB = 0
                            for qs in range(2):
                                qA = qspan[0:64, pr * QSPAN + qs * 512: pr * QSPAN + qs * 512 + 512]
                                c_scA = P.inc("tensor", lambda e, o=scores[:, qs * 512:qs * 512 + 512],
                                              l=kA, r=qA: e.matmul(o, l, r, start=True, stop=True, tile_position=(0, 0)), P.pe)
                            P.wait("tensor", P.act, sigB_cnt.get(kti - 1, 0))
                            for qs in range(2):
                                qB = qspan[64:128, pr * QSPAN + qs * 512: pr * QSPAN + qs * 512 + 512]
                                c_scB = P.inc("tensor", lambda e, o=scores[:, 1024 + qs * 512:1024 + qs * 512 + 512],
                                              l=kB, r=qB, last=(qs == 1): e.matmul(o, l, r, start=True, stop=True, tile_position=(64, 0)), P.pe)
                            scr_cnt[kti] = c_scB
                            P.wait("scalar", P.pe, c_scA)
                            if kti >= 4:
                                P.wait("scalar", P.pe, avz_cnt[kti - 4])
                            wb = wbuf[:, (kti % 4) * 2 * QSPAN:(kti % 4 + 1) * 2 * QSPAN]
                            sigA_cnt[kti] = P.inc("scalar", lambda e, o=wb[:, 0:1024], i=scores[:, 0:1024]: e.activation(
                                o, i, AF.Sigmoid, scale=SCALE), P.act)
                            P.wait("scalar", P.pe, c_scB)
                            sigB_cnt[kti] = P.inc("scalar", lambda e, o=wb[:, 1024:2048], i=scores[:, 1024:2048]: e.activation(
                                o, i, AF.Sigmoid, scale=SCALE), P.act)
                            if len(pend) >= 2:
                                emit_avz(pend.pop(0))
                            pend.append((s, g, pr, k4, kti, gb))
                            kti += 1
                    grp_last[g] = kti - 1
                while pend:
                    emit_avz(pend.pop(0))
                P.wait("sync", P.pe, avz_cnt[kti - 1])
                for hc in range(4, 8):
                    P.load(kg[:, hc * D:(hc + 1) * D], wbo_scr[hc * 128:(hc + 1) * 128, :])
                    c_wt = P.load(vg[:, hc * D:(hc + 1) * D], wbf_scr[hc * 128:(hc + 1) * 128, :])

                c_last_sig = sigB_cnt[kti - 1]
                c_spill = P.dve.n
                w_o_t, w_f_t = kg, vg
                if debug and s == 0:
                    P.wait("gpsimd", P.dve, c_spill)
                    P.store(aacc_dbg[:, :], attn_sb[:, :])
                    P.store(zacc_dbg[:, :], z_sb[:, :])
                c_nrm = tail_gates["c_nrm0"]
                tb0 = dict(bufs); tb0["xbb"] = qspan[:, 0:KC * CH]
                tb1 = dict(bufs); tb1["xbb"] = qspan[:, KC * CH:2 * KC * CH]
                tb1["sqb"] = abf0; tb1["lnb"] = abf1
                P.wait("sync", P.pe, c_tail_pe)
                P.wait("sync", P.G, tail_gates["c_out_st"])
                c_x1l0 = 0
                for hc in range(KC):
                    c_x1l0 = P.load(_hc(xc, hc), x1r[hc][:, q0:q0 + 512])
                P.wait("tensor", P.dve, c_nrm)
                P.wait("tensor", P.act, c_last_sig)
                P.wait("tensor", P.W, c_wt)
                P.wait("tensor", P.G, tail_gates["c_out_st"])
                P.wait("vector", P.G, tail_gates["c_out_st"])
                c_x2a = {}
                for oc in range(KC):
                    Fv = Wo0 if oc % 2 == 0 else Wo1
                    if oc >= 2:
                        P.wait("tensor", P.dve, c_x2a[oc - 2])
                    c_f = 0
                    for hc in range(KC):
                        c_f = P.inc("tensor", lambda e, o=w_o_t[:, hc * D + oc * 128: hc * D + oc * 128 + 128], r=_hc(abf0, hc), oo=Fv,
                                    st=(hc == 0), sp=(hc == KC - 1): e.matmul(oo, o, r, start=st, stop=sp), P.pe)
                    P.wait("vector", P.pe, c_f)
                    P.wait("vector", P.W, c_x1l0)
                    c_x2a[oc] = P.inc("vector", lambda e, o=_hc(x1c, oc), a=_hc(xc, oc), b=Fv:
                                      e.tensor_add(o, a, b), P.dve)
                c_wo0 = P.pe.n
                P.wait("sync", P.dve, c_x2a[KC - 1])
                c_x1l1 = 0
                for hc in range(KC):
                    c_x1l1 = P.load(_hc(xc, hc), x1r[hc][:, q0 + 512:q0 + 1024])
                c_x2b = {}
                for oc in range(KC):
                    Fv = Wo0 if oc % 2 == 0 else Wo1
                    P.wait("tensor", P.dve, c_x2a[KC - 2 + (oc % 2)] if oc < 2 else c_x2b[oc - 2])
                    c_f = 0
                    for hc in range(KC):
                        c_f = P.inc("tensor", lambda e, o=w_o_t[:, hc * D + oc * 128: hc * D + oc * 128 + 128], r=_hc(abf1, hc), oo=Fv,
                                    st=(hc == 0), sp=(hc == KC - 1): e.matmul(oo, o, r, start=st, stop=sp), P.pe)
                    P.wait("vector", P.pe, c_f)
                    P.wait("vector", P.W, c_x1l1)
                    c_x2b[oc] = P.inc("vector", lambda e, o=_hc(xc, oc), a=_hc(xc, oc), b=Fv:
                                      e.tensor_add(o, a, b), P.dve)
                c_wo1 = P.pe.n
                if debug:
                    x2r = x2_dbg.rearrange("(hc p) t -> hc p t", p=128)
                    P.wait("gpsimd", P.dve, c_x2a[KC - 1])
                    for oc in range(KC):
                        P.store(x2r[oc][:, q0:q0 + CH], _hc(x1c, oc))
                    P.wait("gpsimd", P.dve, c_x2b[KC - 1])
                    for oc in range(KC):
                        P.store(x2r[oc][:, q0 + 512:q0 + 1024], _hc(xc, oc))
                r1a = emit_ln(P, x1c, tb0, tail_psums,
                              {"dve": c_x2a[KC - 1], "pe": max(scr_cnt[kti - 1], c_wo0)})
                r2a = emit_ffn(P, lnb, x1c, x1c, w_f_t, tb0, tail_psums, r1a["c_lnb"],
                               c_gfree=tail_gates["c_out_st"])
                r1b = emit_ln(P, xc, tb1, tail_psums,
                              {"dve": max(c_x2b[KC - 1], r1a["c_var"]), "pe": c_wo1})
                r2b = emit_ffn(P, abf1, xc, xc, w_f_t, tb1, tail_psums, r1b["c_lnb"],
                               c_gfree=tail_gates["c_out_st"])
                c_stats3 = r1b["c_stats"]
                P.wait("gpsimd", P.dve, r2a["c_x1"])
                for oc in range(KC):
                    P.store(outr[oc][:, q0:q0 + 512], _hc(x1c, oc))
                P.wait("gpsimd", P.dve, r2b["c_x1"])
                c_out_st = 0
                for oc in range(KC):
                    c_out_st = P.store(outr[oc][:, q0 + 512:q0 + 1024], _hc(xc, oc))
                tail_gates["c_out_st"] = c_out_st
                tail_gates["c_wo_last"] = max(r2a["c_pe_end"], r2b["c_pe_end"])
                c_tail_dve = P.dve.n
                c_tail_pe = P.pe.n
                tail_gates["c_stats3"] = c_stats3

            P.wait("gpsimd", P.G, P.G.n)
            P.run_block()
    return nc


def kernel(L0_attn_x, L0_attn_Wq, L0_attn_Wk, L0_attn_Wv, L0_attn_Wo,
           L0_ffn_weight, L0_ffn_disha):
    x = np.asarray(L0_attn_x, dtype=np.float32)
    wq = np.ascontiguousarray(np.asarray(L0_attn_Wq, dtype=np.float32).T)
    wk = np.ascontiguousarray(np.asarray(L0_attn_Wk, dtype=np.float32).T)
    wv = np.ascontiguousarray(np.asarray(L0_attn_Wv, dtype=np.float32).T)
    wo = np.ascontiguousarray(np.asarray(L0_attn_Wo, dtype=np.float32).T)
    wf = np.ascontiguousarray(np.asarray(L0_ffn_weight, dtype=np.float32).T)
    disha = np.ascontiguousarray(np.asarray(L0_ffn_disha, dtype=np.float32))
    p0 = np.ascontiguousarray(np.tile(np.eye(64, dtype=np.float32), (2, 1)))

    nc = build_nc()
    in_maps = []
    for core in range(8):
        b, h = core // 2, core % 2
        xo = np.ascontiguousarray(x[b, h * QH:(h + 1) * QH, :].T)
        xt = np.ascontiguousarray(x[b, (1 - h) * QH:(2 - h) * QH, :].T)
        in_maps.append({"xfm_own": xo, "xfm_oth": xt, "wtq": wq, "wtk": wk,
                        "wtv": wv, "wto": wo, "wtf": wf, "disha": disha, "p0": p0})
    res = run_bass_kernel_spmd(nc, in_maps, core_ids=list(range(8)))
    out = np.empty((B, S, D), dtype=np.float32)
    for core in range(8):
        b, h = core // 2, core % 2
        out[b, h * QH:(h + 1) * QH, :] = res.results[core]["out_fm"].T
    return out
